# version 04
# baseline (speedup 1.0000x reference)
import numpy as np
import ml_dtypes
import concourse.bass as bass
import concourse.mybir as mybir
import concourse.bass_utils as _bu
from concourse.aot_env import allow_ici_all_to_all
from contextlib import ExitStack

F32 = mybir.dt.float32
BF16 = mybir.dt.bfloat16
AF = mybir.ActivationFunctionType
ALU = mybir.AluOpType
AXX = mybir.AxisListType.X

NC = 8
S = 8192
D = 2048
H = 16
HD = 128
TOWN = S // NC
NTB = TOWN // 128
INTER = 8192
RS = float(1.0 / np.sqrt(HD))
SQ = float(np.sqrt(HD))
CSH = 4.0
EPS = 1e-5
MASKVAL = -1e15

ENGS = ("sync", "scalar", "vector", "tensor", "gpsimd")

_STAGE = 5


class Sem:
    def __init__(self, nc, es, name):
        self.h = es.enter_context(nc.semaphore(name))
        self.n = 0

    def bump(self, k):
        self.n += k
        return self.n


class Prog:
    def __init__(self):
        self.ops = {e: [] for e in ENGS}

    def emit(self, eng, fn):
        self.ops[eng].append(fn)

    def wait(self, eng, sem, val=None):
        v = sem.n if val is None else val
        if v <= 0:
            return
        self.emit(eng, lambda e, s=sem.h, v=v: e.wait_ge(s, v))

    def dma(self, eng, out, in_, sem):
        sem.bump(16)
        self.emit(eng, lambda e, o=out, i=in_, s=sem.h: e.dma_start(out=o, in_=i).then_inc(s, 16))

    def op(self, eng, fn, sem=None, k=1):
        if sem is None:
            self.emit(eng, fn)
        else:
            sem.bump(k)
            self.emit(eng, lambda e, f=fn, s=sem.h, k=k: f(e).then_inc(s, k))


def build_program(stage=None, part="ALL"):
    if stage is None:
        stage = _STAGE
    nc = bass.Bass("TRN2", target_bir_lowering=False, num_devices=NC)
    es = ExitStack()

    def din(name, shape, dt=F32):
        return nc.dram_tensor(name, shape, dt, kind="ExternalInput").ap()

    x_own = din("x_own", [TOWN, D])
    wq_own = din("wq_own", [D, 256])
    wk_own = din("wk_own", [D, 256])
    wv_own = din("wv_own", [D, 256])
    bqk = din("bqk", [128, 4])
    bv_own = din("bv_own", [1, 256])
    bo_in = din("bo_in", [1, D])
    wo_perm = din("wo_perm", [D, D])
    w_gu = din("w_gu", [D, 2 * INTER])
    w_dT = din("w_dT", [INTER, D])
    alibi = din("alibi", [128, 2])
    delta = din("delta", [1088, 1024], BF16)
    masktri = din("masktri", [128, 128], BF16)
    ident_in = din("ident_in", [128, 128], BF16)
    y = nc.dram_tensor("y", [TOWN, D], F32, kind="ExternalOutput").ap()
    dbg = None
    if stage < 5:
        dbg_shape = {1: [2048, TOWN], 2: [640, 16384], 3: [256, S], 4: [128, 16 * TOWN]}[stage]
        dbg = nc.dram_tensor("dbg", dbg_shape, F32, kind="ExternalOutput").ap()

    yctx = nc.dram_tensor("yctx", [NC * 256, TOWN], BF16, kind="ExternalOutput").ap() if part == "A" else None
    ctx_own = din("ctx_own", [NC * 256, TOWN], BF16) if part == "B" else None
    ln1t_in = nc.dram_tensor("ln1t_in", [D, TOWN], BF16, kind="Internal")
    ln1t_out = nc.dram_tensor("ln1t_out", [NC * D, TOWN], BF16, kind="Internal")
    ctx_in = nc.dram_tensor("ctx_in", [NC * 256, TOWN], BF16, kind="Internal")
    ctx_out = nc.dram_tensor("ctx_out", [NC * 256, TOWN], BF16, kind="Internal")
    wo_bfd = nc.dram_tensor("wo_bfd", [D, D], BF16, kind="Internal")
    gu_bfd = nc.dram_tensor("gu_bfd", [D, 2 * INTER], BF16, kind="Internal")
    wd_bfd = nc.dram_tensor("wd_bfd", [INTER, D], BF16, kind="Internal")
    x1_dram = nc.dram_tensor("x1_dram", [TOWN, D], BF16, kind="Internal")

    def sb(name, shape, dt, side=None):
        return es.enter_context(nc.sbuf_tensor(name, shape, dt, side=side))

    alibi_sb = sb("alibi_sb", [128, 2], F32)
    mask_sb = sb("mask_sb", [128, 128], BF16)
    ident_sb = sb("ident_sb", [128, 128], BF16)
    bqk_sb = sb("bqk_sb", [128, 4], F32)
    ones_r = sb("ones_r", [1, 128], BF16)
    ones_c = sb("ones_c", [128, 1], BF16)
    ones_rf = sb("ones_rf", [1, 128], F32)
    delta_rb = sb("delta_rb", [1, 4 * 1024], BF16)
    stat = sb("stat", [128, 8], F32)
    epsv = sb("epsv", [128, 1], F32)
    lrec = sb("lrec", [1, 512], F32)
    sqs = sb("sqs", [128, D], BF16)
    nrmt = sb("nrmt", [128, D], BF16)

    sems = {}

    def S_(name):
        if name not in sems:
            sems[name] = Sem(nc, es, name)
        return sems[name]

    p = Prog()

    def fm3(ap):
        return ap.rearrange("(dc p) c -> p dc c", p=128)

    def sb3(ap, n):
        return ap.rearrange("p (dc c) -> p dc c", dc=n)

    ld_c = S_("ld_c"); ld_w = S_("ld_w"); cs_w = S_("cs_w")
    t_b = S_("t_b"); v_b = S_("v_b"); v_ini = S_("v_ini")
    ld_pc = S_("ld_pc"); cs_pc = S_("cs_pc"); st_pc = S_("st_pc")
    ld_x = S_("ld_x"); v_ln = S_("v_ln"); a_sq = S_("a_sq")
    t_tr = S_("t_tr"); v_te = S_("v_te"); st_l1 = S_("st_l1"); cc = S_("cc")
    v_st = S_("v_st"); a_rs = S_("a_rs")
    ld_g = S_("ld_g"); t_qkv = S_("t_qkv"); v_qe = S_("v_qe")
    s_mm = S_("s_mm"); v_mk = S_("v_mk"); a_ex = S_("a_ex"); s_av = S_("s_av")
    a_rc = S_("a_rc"); t_bc = S_("t_bc"); v_cx = S_("v_cx"); ld_d = S_("ld_d"); st_cx = S_("st_cx")
    ld_cx = S_("ld_cx"); ld_wo = S_("ld_wo"); t_op = S_("t_op"); v_oe = S_("v_oe")
    v_x1 = S_("v_x1"); st_x1 = S_("st_x1"); t_tr2 = S_("t_tr2"); v_te2 = S_("v_te2")
    v_st4 = S_("v_st4"); a_rs4 = S_("a_rs4")
    ld_gu = S_("ld_gu"); ld_wd = S_("ld_wd"); ld_x1 = S_("ld_x1")
    t_gu = S_("t_gu"); a_si = S_("a_si"); v_h = S_("v_h")
    t_dn = S_("t_dn"); v_de = S_("v_de"); t_tr5 = S_("t_tr5"); v_te5 = S_("v_te5")
    v_y = S_("v_y"); st_y = S_("st_y"); v_st5 = S_("v_st5"); a_rs5 = S_("a_rs5")
    st_dbg = S_("st_dbg")
    v_dr = S_("v_dr")

    p.dma("sync", alibi_sb[:, :], alibi, ld_c)
    p.dma("sync", mask_sb[:, :], masktri, ld_c)
    p.dma("sync", ident_sb[:, :], ident_in, ld_c)
    p.dma("sync", bqk_sb[:, :], bqk, ld_c)
    p.op("vector", lambda e: e.memset(ones_r[:, :], 1.0))
    p.op("vector", lambda e: e.memset(ones_c[:, :], 1.0))
    p.op("vector", lambda e: e.memset(ones_rf[:, :], 1.0))
    p.op("vector", lambda e: e.memset(epsv[:, :], float(EPS)), v_ini)

    bot_r = sb("bot_r", [128, D], F32, side="right")
    bvt_r = sb("bvt_r", [128, 256], F32, side="right")
    es_w = ExitStack()
    wqb = es_w.enter_context(nc.sbuf_tensor("wqb", [128, 16 * 256], BF16, side="right"))
    wkb = es_w.enter_context(nc.sbuf_tensor("wkb", [128, 16 * 256], BF16, side="right"))
    wvb = es_w.enter_context(nc.sbuf_tensor("wvb", [128, 16 * 256], BF16, side="right"))

    es_pc = ExitStack()
    pc_stg = es_pc.enter_context(nc.sbuf_tensor("pc_stg", [128, 16 * 128], F32))
    pc_bf = es_pc.enter_context(nc.sbuf_tensor("pc_bf", [128, 16 * 128], BF16))

    with nc.sbuf_tensor("wstg", [128, 16 * 256], F32) as wstg:
        for wi, (src, dst) in enumerate([(wq_own, wqb), (wk_own, wkb), (wv_own, wvb)]):
            src3 = fm3(src)
            if wi > 0:
                p.wait("sync", cs_w, wi)
            p.dma("sync", sb3(wstg[:, :], 16), src3, ld_w)
            p.wait("gpsimd", ld_w, 16 * (wi + 1))
            p.op("gpsimd", lambda e, d=dst: e.tensor_copy(d[:, :], wstg[:, :]), cs_w)

        with (
            nc.sbuf_tensor("bv_f", [1, 256], F32) as bv_f,
            nc.sbuf_tensor("bo_f", [1, D], F32) as bo_f,
            nc.psum_tensor("pb0", [128, 512], F32) as pb0,
            nc.psum_tensor("pb1", [128, 512], F32) as pb1,
        ):
            p.dma("sync", bv_f[:, :], bv_own, ld_c)
            p.dma("sync", bo_f[:, :], bo_in, ld_c)
            p.wait("tensor", ld_c, ld_c.n)
            p.wait("tensor", v_ini, 1)
            p.op("tensor", lambda e: e.matmul(pb0[:, 0:256], ones_rf[:, :], bv_f[:, :], start=True, stop=True), t_b)
            p.wait("vector", t_b, 1)
            p.op("vector", lambda e: e.tensor_copy(bvt_r[:, :], pb0[:, 0:256]), v_b)
            for j in range(4):
                p.op("tensor", lambda e, j=j: e.matmul(pb1[:, :], ones_rf[:, :], bo_f[:, 512 * j : 512 * j + 512], start=True, stop=True), t_b)
                p.wait("vector", t_b, 2 + j)
                p.op("vector", lambda e, j=j: e.tensor_copy(bot_r[:, 512 * j : 512 * j + 512], pb1[:, :]), v_b)
                if j < 3:
                    p.wait("tensor", v_b, 2 + j)

    pcast_jobs = []

    def plan_precast(src, dst, rows, cols, col_blk):
        for c0 in range(0, cols, col_blk):
            w = min(col_blk, cols - c0)
            sview = fm3(src[:, c0 : c0 + w])
            dview = fm3(dst.ap()[:, c0 : c0 + w])
            pcast_jobs.append((sview, dview, (rows // 128) * w, rows // 128))

    plan_precast(wo_perm, wo_bfd, D, D, 128)
    plan_precast(w_gu, gu_bfd, D, 2 * INTER, 128)
    plan_precast(w_dT, wd_bfd, INTER, D, 32)
    NPC_WO, NPC_GU, NPC_WD = 16, 128, 64
    npc = len(pcast_jobs)

    def emit_precast(n):
        for _ in range(n):
            if not pcast_jobs:
                return
            sview, dview, nel, nch = pcast_jobs.pop(0)
            j = emit_precast.j
            emit_precast.j += 1
            p.wait("sync", cs_pc, j)
            p.dma("sync", sb3(pc_stg[:, 0:nel], nch), sview, ld_pc)
            p.wait("gpsimd", ld_pc, 16 * (j + 1))
            p.wait("gpsimd", st_pc, 16 * j)
            p.op("gpsimd", lambda e, n=nel: e.tensor_copy(pc_bf[:, 0:n], pc_stg[:, 0:n]), cs_pc)
            p.wait("sync", cs_pc, j + 1)
            p.dma("sync", dview, sb3(pc_bf[:, 0:nel], nch), st_pc)
    emit_precast.j = 0

    es1 = ExitStack()
    xr = [es1.enter_context(nc.sbuf_tensor(f"xr{i}", [128, D], F32)) for i in range(2)]
    lntm = [es1.enter_context(nc.sbuf_tensor(f"lntm{i}", [128, D], BF16)) for i in range(2)]
    ln1t = es1.enter_context(nc.sbuf_tensor("ln1t", [128, 16 * TOWN], BF16))
    ptr1 = [es1.enter_context(nc.psum_tensor(f"ptr1{i}", [128, 128], BF16)) for i in range(2)]

    def ln_stats(src_tile, d_free, v_done, a_done):
        p.op("vector", lambda e, s=src_tile: e.reduce_sum(stat[:, 0:1], s, axis=AXX), v_dr)
        p.wait("vector", v_dr, v_dr.n)
        p.op("vector", lambda e, d=d_free: e.tensor_scalar_mul(stat[:, 1:2], stat[:, 0:1], 1.0 / d), v_dr)
        p.op("scalar", lambda e, s=src_tile, d=d_free: e.activation(sqs[:, 0:d], s, AF.Square, accum_out=stat[:, 5:6]), a_sq)
        p.wait("vector", a_sq, a_sq.n)
        p.wait("vector", v_dr, v_dr.n)
        p.op("vector", lambda e: e.tensor_scalar(stat[:, 7:8], stat[:, 0:1], stat[:, 1:2], None, op0=ALU.mult), v_dr)
        p.wait("vector", v_dr, v_dr.n)
        p.op("vector", lambda e: e.tensor_sub(stat[:, 2:3], stat[:, 7:8], stat[:, 5:6]), v_done)
        p.wait("scalar", v_done, v_done.n)
        p.op("scalar", lambda e, d=d_free: e.activation(stat[:, 6:7], stat[:, 2:3], AF.Sqrt, bias=epsv[:, 0:1], scale=-1.0 / d), a_done)
        p.wait("vector", a_done, a_done.n)
        p.op("vector", lambda e: e.reciprocal(stat[:, 3:4], stat[:, 6:7]), v_dr)
        p.wait("vector", v_dr, v_dr.n)
        p.op("vector", lambda e: e.tensor_scalar(stat[:, 4:5], stat[:, 1:2], stat[:, 3:4], -1.0, op0=ALU.mult, op1=ALU.mult), v_done)
        p.wait("vector", v_done, v_done.n)

    p.wait("vector", v_b, 5)
    p.wait("tensor", v_b, 5)
    p.wait("vector", cs_w, 3)
    p.wait("sync", cs_w, 3)
    p.wait("sync", t_b, 5)

    for tb in range(NTB):
        b = tb % 2
        if tb >= 2:
            p.wait("sync", v_ln, tb - 1)
        p.dma("sync", xr[b][:, :], x_own[128 * tb : 128 * tb + 128, :], ld_x)
        p.wait("vector", ld_x, 16 * (tb + 1))
        p.wait("scalar", ld_x, 16 * (tb + 1))
        ln_stats(xr[b][:, :], D, v_st, a_rs)
        if tb >= 2:
            p.wait("vector", t_tr, 16 * (tb - 1))
        p.op("vector", lambda e, b=b: e.tensor_scalar(lntm[b][:, :], xr[b][:, :], stat[:, 3:4], stat[:, 4:5], op0=ALU.mult, op1=ALU.add), v_ln)
        p.wait("tensor", v_ln, v_ln.n)
        for dc in range(16):
            pb = dc % 2
            if t_tr.n >= 2:
                p.wait("tensor", v_te, v_te.n - 1)
            p.op("tensor", lambda e, b=b, dc=dc, pb=pb: e.transpose(ptr1[pb][:, :], lntm[b][:, 128 * dc : 128 * dc + 128], ident_sb[:, :]), t_tr)
            p.wait("vector", t_tr, t_tr.n)
            p.op("vector", lambda e, dc=dc, tb=tb, pb=pb: e.tensor_copy(ln1t[:, dc * TOWN + 128 * tb : dc * TOWN + 128 * tb + 128], ptr1[pb][:, :]), v_te)
    p.wait("scalar", v_te, v_te.n)
    for dc in range(16):
        p.dma("scalar", ln1t_in.ap()[128 * dc : 128 * dc + 128, :], ln1t[:, dc * TOWN : (dc + 1) * TOWN], st_l1)
    p.wait("gpsimd", st_l1, st_l1.n)
    p.op("gpsimd", lambda e: e.collective_compute(
        "AllGather", ALU.bypass, replica_groups=[list(range(NC))],
        ins=[ln1t_in.ap().opt()], outs=[ln1t_out.ap().opt()]), cc)

    emit_precast(32)

    if stage == 1:
        with nc.sbuf_tensor("dbf", [128, TOWN], F32) as dbf:
            vdb = S_("vdb")
            for dc in range(16):
                p.wait("vector", st_dbg, 16 * dc)
                p.op("vector", lambda e, dc=dc: e.tensor_copy(dbf[:, :], ln1t[:, dc * TOWN : (dc + 1) * TOWN]), vdb)
                p.wait("scalar", vdb, vdb.n)
                p.dma("scalar", dbg[128 * dc : 128 * dc + 128, 0:TOWN], dbf[:, :], st_dbg)
        p.wait("sync", st_dbg, st_dbg.n)
        return _finish(nc, p, es, es1, es_pc, es_w)

    es1.close()
    es23 = ExitStack()
    qsb = [es23.enter_context(nc.sbuf_tensor(f"qsb{hl}", [128, S], BF16)) for hl in range(2)]
    ksb = [es23.enter_context(nc.sbuf_tensor(f"ksb{hl}", [128, S], BF16)) for hl in range(2)]
    vsb = es23.enter_context(nc.sbuf_tensor("vsb", [128, 64 * 256], BF16))
    es2 = ExitStack()
    lng = [es2.enter_context(nc.sbuf_tensor(f"lng{i}", [128, 16 * 512], BF16)) for i in range(2)]
    pq = [es2.enter_context(nc.psum_tensor(f"pq{i}", [128, 512], F32)) for i in range(2)]
    pv = [es2.enter_context(nc.psum_tensor(f"pv{i}", [128, 512], F32)) for i in range(2)]

    p.wait("sync", cc, 1)
    p.wait("tensor", cs_w, 3)
    p.wait("tensor", v_te, v_te.n)
    for g in range(16):
        b = g % 2
        rank, half = g // 2, g % 2
        if g >= 2:
            p.wait("sync", t_qkv, t_qkv.n)
        src = ln1t_out.ap()[D * rank : D * rank + D, 512 * half : 512 * half + 512]
        p.dma("sync", sb3(lng[b][:, :], 16), fm3(src), ld_g)
        p.wait("tensor", ld_g, 16 * (g + 1))
        for which, (wsb, qk, bcol0) in enumerate([(wqb, qsb, 0), (wkb, ksb, 2)]):
            for hl in range(2):
                pb = (which * 2 + hl) % 2
                if v_qe.n >= 2:
                    p.wait("tensor", v_qe, v_qe.n - 1)
                for dc in range(16):
                    p.op("tensor", lambda e, w=wsb, dc=dc, hl=hl, b=b, pb=pb: e.matmul(
                        pq[pb][:, :], w[:, dc * 256 + hl * 128 : dc * 256 + hl * 128 + 128],
                        lng[b][:, dc * 512 : dc * 512 + 512],
                        start=(dc == 0), stop=(dc == 15)), t_qkv if dc == 15 else None)
                p.wait("vector", t_qkv, t_qkv.n)
                p.op("vector", lambda e, qk=qk, hl=hl, g=g, pb=pb, bcol=bcol0 + hl: e.tensor_scalar(
                    qk[hl][:, 512 * g : 512 * g + 512], pq[pb][:, :], bqk_sb[:, bcol : bcol + 1], None, op0=ALU.add), v_qe)
        for t4 in range(4):
            pb = t4 % 2
            if v_qe.n >= 2:
                p.wait("tensor", v_qe, v_qe.n - 1)
            for dc in range(16):
                p.op("tensor", lambda e, b=b, dc=dc, t4=t4, pb=pb: e.matmul(
                    pv[pb][:, 0:256], lng[b][:, dc * 512 + 128 * t4 : dc * 512 + 128 * t4 + 128],
                    wvb[:, dc * 256 : dc * 256 + 256],
                    start=(dc == 0), stop=(dc == 15)), t_qkv if dc == 15 else None)
            p.wait("vector", t_qkv, t_qkv.n)
            tok = 512 * g + 128 * t4
            p.op("vector", lambda e, tok=tok, pb=pb: e.tensor_add(
                vsb[:, tok * 2 : tok * 2 + 256], pv[pb][:, 0:256], bvt_r[:, :]), v_qe)
        if g == 7:
            emit_precast(32)

    emit_precast(32)

    if stage == 2:
        with nc.sbuf_tensor("dbf", [128, 2048], F32) as dbf:
            vdb = S_("vdb")
            p.wait("vector", v_qe, v_qe.n)
            for i, (t, w) in enumerate([(qsb[0], S), (qsb[1], S), (ksb[0], S), (ksb[1], S), (vsb, 16384)]):
                for cchunk in range(0, w, 2048):
                    p.wait("vector", st_dbg, st_dbg.n)
                    p.op("vector", lambda e, t=t, c=cchunk: e.tensor_copy(dbf[:, :], t[:, c : c + 2048]), vdb)
                    p.wait("scalar", vdb, vdb.n)
                    p.dma("scalar", dbg[128 * i : 128 * i + 128, cchunk : cchunk + 2048], dbf[:, :], st_dbg)
        p.wait("sync", st_dbg, st_dbg.n)
        return _finish(nc, p, es, es2, es23, es_pc, es_w)

    es_w.close()
    es2.close()
    es3 = ExitStack()
    pp = [es3.enter_context(nc.sbuf_tensor(f"pp{i}", [128, 512], BF16)) for i in range(2)]
    cxe = [es3.enter_context(nc.sbuf_tensor(f"cxe{i}", [128, 512], BF16)) for i in range(2)]
    lbc = es3.enter_context(nc.sbuf_tensor("lbc", [128, 512], F32))
    ps_ = [es3.enter_context(nc.psum_tensor(f"ps{i}", [128, 512], F32)) for i in range(2)]
    pctx = [es3.enter_context(nc.psum_tensor(f"pctx{i}", [128, 512], F32)) for i in range(2)]
    pl_ = [es3.enter_context(nc.psum_tensor(f"pl{i}", [1, 512], F32)) for i in range(2)]
    pbc = es3.enter_context(nc.psum_tensor("pbc", [128, 512], F32))

    p.wait("tensor", v_qe, v_qe.n)
    kt_g = 0
    ch_g = 0
    exp_of = {}
    av_of = {}
    smm_of = {}
    drow = 0
    for hl in range(2):
        for c in range(16):
            cb = ch_g % 2
            nkt = 4 * (c + 1)
            if ch_g >= 1:
                p.wait("tensor", v_cx, ch_g)
            for kt in range(nkt):
                sb_ = kt_g % 2
                dslot = kt_g % 4
                j = kt - 4 * c
                col0 = 128 * j if j >= 0 else 0
                if kt_g >= 4:
                    p.wait("sync", s_mm, smm_of[kt_g - 4])
                p.dma("sync", delta_rb[0:1, 1024 * dslot : 1024 * dslot + 1024], delta[drow : drow + 1, :], ld_d)
                drow += 1
                if kt_g >= 2:
                    p.wait("tensor", a_ex, exp_of[kt_g - 2])
                p.wait("tensor", ld_d, 16 * kt_g + 16)
                p.op("tensor", lambda e, hl=hl, kt=kt, c=c, sb_=sb_: e.matmul(
                    ps_[sb_][:, :], ksb[hl][:, 128 * kt : 128 * kt + 128],
                    qsb[hl][:, 512 * c : 512 * c + 512], start=True, stop=False))
                p.op("tensor", lambda e, sb_=sb_, ds=dslot: e.matmul(
                    ps_[sb_][:, :], ones_r[:, :], delta_rb[0:1, 1024 * ds : 1024 * ds + 512],
                    start=False, stop=False))
                p.op("tensor", lambda e, sb_=sb_, ds=dslot, sp=(j < 0): e.matmul(
                    ps_[sb_][:, :], ones_r[:, :], delta_rb[0:1, 1024 * ds + 512 : 1024 * ds + 1024],
                    start=False, stop=(j < 0)), s_mm if j < 0 else None)
                if j >= 0:
                    p.op("tensor", lambda e, sb_=sb_, col0=col0: e.matmul(
                        ps_[sb_][:, col0 : col0 + 128], mask_sb[:, :], ident_sb[:, :],
                        start=False, stop=True), s_mm)
                smm_of[kt_g] = s_mm.n
                p.wait("scalar", s_mm, s_mm.n)
                if kt_g >= 2:
                    p.wait("scalar", s_av, av_of[kt_g - 2])
                p.op("scalar", lambda e, sb_=sb_, col0=col0, hl=hl: e.activation(
                    pp[sb_][:, col0:512], ps_[sb_][:, col0:512], AF.Exp,
                    bias=alibi_sb[:, hl : hl + 1], scale=RS), a_ex)
                exp_of[kt_g] = a_ex.n
                p.wait("tensor", a_ex, a_ex.n)
                p.op("tensor", lambda e, kt=kt, hl=hl, cb=cb, sb_=sb_, col0=col0, st=(kt == 0), sp=(kt == nkt - 1): e.matmul(
                    pctx[cb][:, col0:512], vsb[:, 256 * kt + 128 * hl : 256 * kt + 128 * hl + 128],
                    pp[sb_][:, col0:512], start=st, stop=sp))
                p.op("tensor", lambda e, cb=cb, sb_=sb_, col0=col0, st=(kt == 0), sp=(kt == nkt - 1): e.matmul(
                    pl_[cb][0:1, col0:512], ones_c[:, :], pp[sb_][:, col0:512], start=st, stop=sp), s_av)
                av_of[kt_g] = s_av.n
                kt_g += 1
            p.wait("vector", s_av, s_av.n)
            if ch_g >= 1:
                p.wait("vector", t_bc, t_bc.n)
            p.op("vector", lambda e, cb=cb: e.reciprocal(lrec[0:1, :], pl_[cb][0:1, :]), a_rc)
            if ch_g >= 1:
                p.wait("tensor", v_cx, ch_g)
            p.wait("tensor", a_rc, a_rc.n)
            p.op("tensor", lambda e: e.matmul(pbc[:, :], ones_rf[:, :], lrec[0:1, :], start=True, stop=True), t_bc)
            p.wait("vector", t_bc, t_bc.n)
            if ch_g >= 2:
                p.wait("vector", st_cx, 16 * (ch_g - 1))
            p.op("vector", lambda e: e.tensor_copy(lbc[:, :], pbc[:, :]))
            p.op("vector", lambda e, cb=cb: e.tensor_tensor(cxe[cb][:, :], pctx[cb][:, :], lbc[:, :], op=ALU.mult), v_cx)
            p.wait("scalar", v_cx, v_cx.n)
            ctx_dst = yctx if part == "A" else ctx_in.ap()
            p.dma("scalar", ctx_dst[256 * (c // 2) + 128 * hl : 256 * (c // 2) + 128 * hl + 128,
                                    512 * (c % 2) : 512 * (c % 2) + 512], cxe[cb][:, :], st_cx)
            ch_g += 1
            if ch_g == 16:
                emit_precast(npc)
    if part == "A":
        p.wait("sync", st_cx, st_cx.n)
        return _finish(nc, p, es, es3, es23, es_pc)
    p.wait("gpsimd", st_cx, st_cx.n)
    p.op("gpsimd", lambda e: e.collective_compute(
        "AllToAll", ALU.bypass, replica_groups=[list(range(NC))],
        ins=[ctx_in.ap().opt()], outs=[ctx_out.ap().opt()]), cc)

    if stage == 3:
        p.wait("sync", st_cx, st_cx.n)
        with nc.sbuf_tensor("dbf", [128, TOWN], F32) as dbf, nc.sbuf_tensor("dbb", [128, TOWN], BF16) as dbb:
            vdb = S_("vdb")
            lddb = S_("lddb")
            for i in range(2):
                for j in range(NC):
                    k = i * NC + j
                    if k > 0:
                        p.wait("sync", S_("vdb"), k)
                    p.dma("sync", dbb[:, :], ctx_in.ap()[256 * j + 128 * i : 256 * j + 128 * i + 128, :], lddb)
                    p.wait("vector", lddb, 16 * (k + 1))
                    p.wait("vector", st_dbg, 16 * k)
                    p.op("vector", lambda e: e.tensor_copy(dbf[:, :], dbb[:, :]), vdb)
                    p.wait("scalar", vdb, vdb.n)
                    p.dma("scalar", dbg[128 * i : 128 * i + 128, TOWN * j : TOWN * j + TOWN], dbf[:, :], st_dbg)
        p.wait("sync", st_dbg, st_dbg.n)
        return _finish(nc, p, es, es3, es23, es_pc)

    es3.close()
    es23.close()
    es_pc.close()
    ln2t = sb("ln2t", [128, 16 * TOWN], BF16, side="right")
    es4 = ExitStack()
    xr = [es4.enter_context(nc.sbuf_tensor(f"xr4{i}", [128, D], F32)) for i in range(2)]
    ctxs = es4.enter_context(nc.sbuf_tensor("ctxs", [128, 16 * TOWN], BF16))
    wob = es4.enter_context(nc.sbuf_tensor("wob", [128, 16 * 512], BF16))
    attn = es4.enter_context(nc.sbuf_tensor("attn", [128, NTB * D], BF16))
    x1sb = [es4.enter_context(nc.sbuf_tensor(f"x1sb{i}", [128, D], BF16)) for i in range(2)]
    lntm2 = es4.enter_context(nc.sbuf_tensor("lntm2", [128, D], BF16))
    es4o = ExitStack()
    po_ = [es4o.enter_context(nc.psum_tensor(f"po{i}", [128, 512], F32)) for i in range(2)]

    p.wait("sync", cc, 2)
    p.wait("sync", st_pc, 16 * npc)

    for dc in range(16):
        p.dma("sync", ctxs[:, TOWN * dc : TOWN * dc + TOWN],
              ctx_out.ap()[128 * dc : 128 * dc + 128, :], ld_cx)

    p.wait("tensor", v_cx, 32)
    for dspan in range(4):
        if dspan > 0:
            p.wait("sync", t_op, t_op.n)
            p.wait("tensor", v_oe, v_oe.n)
        p.dma("sync", sb3(wob[:, :], 16), fm3(wo_bfd.ap()[:, 512 * dspan : 512 * dspan + 512]), ld_wo)
        p.wait("tensor", ld_wo, 16 * (dspan + 1))
        if dspan == 0:
            p.wait("tensor", ld_cx, ld_cx.n)
        for tb in range(NTB):
            pb = tb % 2
            if v_oe.n >= 2:
                p.wait("tensor", v_oe, v_oe.n - 1)
            for dc in range(16):
                p.op("tensor", lambda e, dc=dc, tb=tb, pb=pb: e.matmul(
                    po_[pb][:, :], ctxs[:, TOWN * dc + 128 * tb : TOWN * dc + 128 * tb + 128],
                    wob[:, 512 * dc : 512 * dc + 512],
                    start=(dc == 0), stop=(dc == 15)), t_op if dc == 15 else None)
            p.wait("vector", t_op, t_op.n)
            p.op("vector", lambda e, tb=tb, dspan=dspan, pb=pb: e.tensor_add(
                attn[:, D * tb + 512 * dspan : D * tb + 512 * dspan + 512], po_[pb][:, :], bot_r[:, 512 * dspan : 512 * dspan + 512]), v_oe)

    p.wait("sync", v_ln, 8)
    es4o.close()
    es4p = ExitStack()
    ptr2 = [es4p.enter_context(nc.psum_tensor(f"ptr2{i}", [128, 128], BF16)) for i in range(2)]
    for tb in range(NTB):
        b = tb % 2
        if tb >= 2:
            p.wait("sync", v_x1, 2 * (tb - 1))
        p.dma("sync", xr[b][:, :], x_own[128 * tb : 128 * tb + 128, :], ld_x)
        at = attn[:, D * tb : D * tb + D]
        p.wait("vector", v_oe, v_oe.n)
        p.wait("scalar", v_oe, v_oe.n)
        ln_stats(at, D, v_st4, a_rs4)
        p.wait("vector", ld_x, 16 * (NTB + tb + 1))
        if tb >= 2:
            p.wait("vector", st_x1, 16 * (tb - 1))
        p.op("vector", lambda e, at=at: e.tensor_scalar(nrmt[:, :], at, stat[:, 3:4], stat[:, 4:5], op0=ALU.mult, op1=ALU.add))
        p.op("vector", lambda e, b=b: e.tensor_add(x1sb[b][:, :], nrmt[:, :], xr[b][:, :]), v_x1)
        p.wait("scalar", v_x1, v_x1.n)
        p.dma("scalar", x1_dram.ap()[128 * tb : 128 * tb + 128, :], x1sb[b][:, :], st_x1)
        p.wait("scalar", v_x1, v_x1.n)
        ln_stats(x1sb[b][:, :], D, v_st4, a_rs4)
        p.op("vector", lambda e, b=b: e.tensor_scalar(lntm2[:, :], x1sb[b][:, :], stat[:, 3:4], stat[:, 4:5], op0=ALU.mult, op1=ALU.add), v_x1)
        p.wait("tensor", v_x1, v_x1.n)
        if tb == 0:
            p.wait("tensor", v_oe, v_oe.n)
        for dc in range(16):
            pb = dc % 2
            if t_tr2.n >= 2:
                p.wait("tensor", v_te2, v_te2.n - 1)
            p.op("tensor", lambda e, dc=dc, pb=pb: e.transpose(ptr2[pb][:, :], lntm2[:, 128 * dc : 128 * dc + 128], ident_sb[:, :]), t_tr2)
            p.wait("vector", t_tr2, t_tr2.n)
            p.op("vector", lambda e, dc=dc, tb=tb, pb=pb: e.tensor_copy(ln2t[:, dc * TOWN + 128 * tb : dc * TOWN + 128 * tb + 128], ptr2[pb][:, :]), v_te2)

    if stage == 4:
        with nc.sbuf_tensor("dbf", [128, 2048], F32) as dbf:
            vdb = S_("vdb")
            p.wait("vector", v_te2, v_te2.n)
            for cc_ in range(0, 16 * TOWN, 2048):
                p.wait("vector", st_dbg, st_dbg.n)
                p.op("vector", lambda e, c=cc_: e.tensor_copy(dbf[:, :], ln2t[:, c : c + 2048]), vdb)
                p.wait("scalar", vdb, vdb.n)
                p.dma("scalar", dbg[0:128, cc_ : cc_ + 2048], dbf[:, :], st_dbg)
        p.wait("sync", st_dbg, st_dbg.n)
        return _finish(nc, p, es, es4p, es4)

    es4p.close()
    es4.close()
    es5 = ExitStack()
    hsb = es5.enter_context(nc.sbuf_tensor("hsb", [128, 64 * 512], BF16))
    gur = [es5.enter_context(nc.sbuf_tensor(f"gur{i}", [128, 16 * 256], BF16)) for i in range(2)]
    wdr = [es5.enter_context(nc.sbuf_tensor(f"wdr{i}", [128, 32 * 128], BF16)) for i in range(2)]
    sil = es5.enter_context(nc.sbuf_tensor("sil", [128, 512], F32))
    ffm = [es5.enter_context(nc.sbuf_tensor(f"ffm{i}", [128, 512], BF16)) for i in range(2)]
    ftm = es5.enter_context(nc.sbuf_tensor("ftm", [128, 4 * D], BF16))
    x1r = [es5.enter_context(nc.sbuf_tensor(f"x1r{i}", [128, D], BF16)) for i in range(2)]
    yt = [es5.enter_context(nc.sbuf_tensor("yt0", [128, D], F32))] * 2
    pg = es5.enter_context(nc.psum_tensor("pg", [128, 512], F32))
    pu = es5.enter_context(nc.psum_tensor("pu", [128, 512], F32))
    pd = [es5.enter_context(nc.psum_tensor(f"pd{i}", [128, 512], F32)) for i in range(2)]
    ptr5 = [es5.enter_context(nc.psum_tensor(f"ptr5{i}", [128, 128], BF16)) for i in range(2)]

    p.wait("sync", v_te2, v_te2.n)
    p.wait("tensor", v_te2, v_te2.n)
    for span in range(2):
        for ic in range(64):
            b = ic % 2
            if span * 64 + ic >= 2:
                p.wait("sync", t_gu, t_gu.n)
            p.dma("sync", sb3(gur[b][:, :], 16), fm3(gu_bfd.ap()[:, 256 * ic : 256 * ic + 256]), ld_gu)
            p.wait("tensor", ld_gu, 16 * (span * 64 + ic + 1))
            if span > 0 and ic == 0:
                p.wait("vector", t_dn, t_dn.n)
            if span * 64 + ic >= 1:
                p.wait("tensor", v_h, v_h.n)
            for dc in range(16):
                p.op("tensor", lambda e, b=b, dc=dc, span=span: e.matmul(
                    pg[:, :], gur[b][:, 256 * dc : 256 * dc + 128],
                    ln2t[:, dc * TOWN + 512 * span : dc * TOWN + 512 * span + 512],
                    start=(dc == 0), stop=(dc == 15)))
            for dc in range(16):
                p.op("tensor", lambda e, b=b, dc=dc, span=span: e.matmul(
                    pu[:, :], gur[b][:, 256 * dc + 128 : 256 * dc + 256],
                    ln2t[:, dc * TOWN + 512 * span : dc * TOWN + 512 * span + 512],
                    start=(dc == 0), stop=(dc == 15)), t_gu if dc == 15 else None)
            p.wait("scalar", t_gu, t_gu.n)
            p.op("scalar", lambda e: e.activation(sil[:, :], pg[:, :], AF.Silu), a_si)
            p.wait("vector", a_si, a_si.n)
            p.op("vector", lambda e, ic=ic: e.tensor_tensor(hsb[:, 512 * ic : 512 * ic + 512], sil[:, :], pu[:, :], op=ALU.mult), v_h)
        if span > 0:
            p.wait("vector", a_rs5, a_rs5.n)
            p.wait("vector", v_y, v_y.n)
        for dc in range(16):
            db = dc % 2
            if span * 16 + dc >= 1:
                p.wait("tensor", v_de, v_de.n)
            for half in range(2):
                hidx = (span * 16 + dc) * 2 + half
                b = hidx % 2
                if hidx >= 2:
                    p.wait("sync", t_dn, t_dn.n)
                p.dma("sync", sb3(wdr[b][:, :], 32),
                      fm3(wd_bfd.ap()[4096 * half : 4096 * half + 4096, 128 * dc : 128 * dc + 128]), ld_wd)
                p.wait("tensor", ld_wd, 16 * (hidx + 1))
                for ic in range(32):
                    icg = 32 * half + ic
                    p.op("tensor", lambda e, b=b, ic=ic, icg=icg, db=db: e.matmul(
                        pd[db][:, :], wdr[b][:, 128 * ic : 128 * ic + 128],
                        hsb[:, 512 * icg : 512 * icg + 512],
                        start=(icg == 0), stop=(icg == 63)), t_dn if icg == 63 else None)
            p.wait("vector", t_dn, t_dn.n)
            if t_tr5.n >= 8:
                p.wait("vector", t_tr5, t_tr5.n - 4)
            p.op("vector", lambda e, b=b, db=dc % 2: e.tensor_copy(ffm[b][:, :], pd[db][:, :]), v_de)
            p.wait("tensor", v_de, v_de.n)
            for j in range(4):
                pb = j % 2
                if t_tr5.n >= 2:
                    p.wait("tensor", v_te5, v_te5.n - 1)
                p.op("tensor", lambda e, b=b, j=j, pb=pb: e.transpose(ptr5[pb][:, :], ffm[b][:, 128 * j : 128 * j + 128], ident_sb[:, :]), t_tr5)
                p.wait("vector", t_tr5, t_tr5.n)
                p.op("vector", lambda e, j=j, dc=dc, pb=pb: e.tensor_copy(ftm[:, D * j + 128 * dc : D * j + 128 * dc + 128], ptr5[pb][:, :]), v_te5)
        for t4 in range(4):
            tb = 4 * span + t4
            b = t4 % 2
            if tb >= 2:
                p.wait("sync", v_y, v_y.n - 1)
            p.wait("sync", st_x1, 16 * (tb + 1))
            p.dma("sync", x1r[b][:, :], x1_dram.ap()[128 * tb : 128 * tb + 128, :], ld_x1)
            ft = ftm[:, D * t4 : D * t4 + D]
            p.wait("vector", v_te5, v_te5.n)
            p.wait("scalar", v_te5, v_te5.n)
            ln_stats(ft, D, v_st5, a_rs5)
            p.wait("vector", ld_x1, 16 * (tb + 1))
            if tb >= 1:
                p.wait("vector", st_y, 16 * tb)
            p.op("vector", lambda e, ft=ft: e.tensor_scalar(nrmt[:, :], ft, stat[:, 3:4], stat[:, 4:5], op0=ALU.mult, op1=ALU.add))
            p.op("vector", lambda e, b=b: e.tensor_add(yt[b][:, :], nrmt[:, :], x1r[b][:, :]), v_y)
            p.wait("scalar", v_y, v_y.n)
            p.dma("scalar", y[128 * tb : 128 * tb + 128, :], yt[b][:, :], st_y)

    p.wait("sync", st_y, st_y.n)
    return _finish(nc, p, es, es5)


def build_b():
    nc = bass.Bass("TRN2", target_bir_lowering=False, num_devices=NC)
    es = ExitStack()

    def din(name, shape, dt=F32):
        return nc.dram_tensor(name, shape, dt, kind="ExternalInput").ap()

    x_own = din("x_own", [TOWN, D])
    ctx_own = din("ctx_own", [NC * 256, TOWN], BF16)
    bo_in = din("bo_in", [1, D])
    wo_perm = din("wo_perm", [D, D])
    w_gu = din("w_gu", [D, 2 * INTER])
    w_dT = din("w_dT", [INTER, D])
    ident_in = din("ident_in", [128, 128], BF16)
    y = nc.dram_tensor("y", [TOWN, D], F32, kind="ExternalOutput").ap()

    wo_bfd = nc.dram_tensor("wo_bfd", [D, D], BF16, kind="Internal")
    gu_bfd = nc.dram_tensor("gu_bfd", [D, 2 * INTER], BF16, kind="Internal")
    wd_bfd = nc.dram_tensor("wd_bfd", [INTER, D], BF16, kind="Internal")
    x1_dram = nc.dram_tensor("x1_dram", [TOWN, D], BF16, kind="Internal")

    def sb(name, shape, dt, side=None):
        return es.enter_context(nc.sbuf_tensor(name, shape, dt, side=side))

    ident_sb = sb("ident_sb", [128, 128], BF16)
    ones_rf = sb("ones_rf", [1, 128], F32)
    stat = sb("stat", [128, 8], F32)
    epsv = sb("epsv", [128, 1], F32)
    sqs = sb("sqs", [128, D], BF16)
    nrmt = sb("nrmt", [128, D], BF16)
    bot_r = sb("bot_r", [128, D], F32, side="right")
    ln2t = sb("ln2t", [128, 16 * TOWN], BF16, side="right")

    sems = {}

    def S_(name):
        if name not in sems:
            sems[name] = Sem(nc, es, name)
        return sems[name]

    p = Prog()

    def fm3(ap):
        return ap.rearrange("(dc p) c -> p dc c", p=128)

    def sb3(ap, n):
        return ap.rearrange("p (dc c) -> p dc c", dc=n)

    ld_c = S_("ld_c"); v_ini = S_("v_ini"); t_b = S_("t_b"); v_b = S_("v_b")
    ld_pc = S_("ld_pc"); cs_pc = S_("cs_pc"); st_pc = S_("st_pc")
    ld_x = S_("ld_x"); a_sq = S_("a_sq"); v_dr = S_("v_dr")
    ld_cx = S_("ld_cx"); ld_wo = S_("ld_wo"); t_op = S_("t_op"); v_oe = S_("v_oe")
    v_x1 = S_("v_x1"); st_x1 = S_("st_x1"); t_tr2 = S_("t_tr2"); v_te2 = S_("v_te2")
    v_st4 = S_("v_st4"); a_rs4 = S_("a_rs4")
    ld_gu = S_("ld_gu"); ld_wd = S_("ld_wd"); ld_x1 = S_("ld_x1")
    t_gu = S_("t_gu"); a_si = S_("a_si"); v_h = S_("v_h")
    t_dn = S_("t_dn"); v_de = S_("v_de"); t_tr5 = S_("t_tr5"); v_te5 = S_("v_te5")
    v_y = S_("v_y"); st_y = S_("st_y"); v_st5 = S_("v_st5"); a_rs5 = S_("a_rs5")

    def ln_stats(src_tile, d_free, v_done, a_done):
        p.op("vector", lambda e, s=src_tile: e.reduce_sum(stat[:, 0:1], s, axis=AXX), v_dr)
        p.wait("vector", v_dr, v_dr.n)
        p.op("vector", lambda e, d=d_free: e.tensor_scalar_mul(stat[:, 1:2], stat[:, 0:1], 1.0 / d), v_dr)
        p.op("scalar", lambda e, s=src_tile, d=d_free: e.activation(sqs[:, 0:d], s, AF.Square, accum_out=stat[:, 5:6]), a_sq)
        p.wait("vector", a_sq, a_sq.n)
        p.wait("vector", v_dr, v_dr.n)
        p.op("vector", lambda e: e.tensor_scalar(stat[:, 7:8], stat[:, 0:1], stat[:, 1:2], None, op0=ALU.mult), v_dr)
        p.wait("vector", v_dr, v_dr.n)
        p.op("vector", lambda e: e.tensor_sub(stat[:, 2:3], stat[:, 7:8], stat[:, 5:6]), v_done)
        p.wait("scalar", v_done, v_done.n)
        p.op("scalar", lambda e, d=d_free: e.activation(stat[:, 6:7], stat[:, 2:3], AF.Sqrt, bias=epsv[:, 0:1], scale=-1.0 / d), a_done)
        p.wait("vector", a_done, a_done.n)
        p.op("vector", lambda e: e.reciprocal(stat[:, 3:4], stat[:, 6:7]), v_dr)
        p.wait("vector", v_dr, v_dr.n)
        p.op("vector", lambda e: e.tensor_scalar(stat[:, 4:5], stat[:, 1:2], stat[:, 3:4], -1.0, op0=ALU.mult, op1=ALU.mult), v_done)
        p.wait("vector", v_done, v_done.n)

    p.dma("sync", ident_sb[:, :], ident_in, ld_c)
    p.op("vector", lambda e: e.memset(ones_rf[:, :], 1.0))
    p.op("vector", lambda e: e.memset(epsv[:, :], float(EPS)), v_ini)
    with (
        nc.sbuf_tensor("bo_f", [1, D], F32) as bo_f,
        nc.psum_tensor("pb1", [128, 512], F32) as pb1,
    ):
        p.dma("sync", bo_f[:, :], bo_in, ld_c)
        p.wait("tensor", ld_c, ld_c.n)
        p.wait("tensor", v_ini, 1)
        for j in range(4):
            p.op("tensor", lambda e, j=j: e.matmul(pb1[:, :], ones_rf[:, :], bo_f[:, 512 * j : 512 * j + 512], start=True, stop=True), t_b)
            p.wait("vector", t_b, 1 + j)
            p.op("vector", lambda e, j=j: e.tensor_copy(bot_r[:, 512 * j : 512 * j + 512], pb1[:, :]), v_b)
            if j < 3:
                p.wait("tensor", v_b, 1 + j)

    es_pc = ExitStack()
    pc_stg = es_pc.enter_context(nc.sbuf_tensor("pc_stg", [128, 16 * 128], F32))
    pc_bf = es_pc.enter_context(nc.sbuf_tensor("pc_bf", [128, 16 * 128], BF16))
    pcast_jobs = []

    def plan_precast(src, dst, rows, cols, col_blk):
        for c0 in range(0, cols, col_blk):
            w = min(col_blk, cols - c0)
            pcast_jobs.append((fm3(src[:, c0 : c0 + w]), fm3(dst.ap()[:, c0 : c0 + w]), (rows // 128) * w, rows // 128))

    plan_precast(wo_perm, wo_bfd, D, D, 128)
    plan_precast(w_gu, gu_bfd, D, 2 * INTER, 128)
    plan_precast(w_dT, wd_bfd, INTER, D, 32)
    NPC_WO = 16
    npc = len(pcast_jobs)
    p.wait("sync", t_b, 4)
    for j, (sview, dview, nel, nch) in enumerate(pcast_jobs):
        p.wait("sync", cs_pc, j)
        p.dma("sync", sb3(pc_stg[:, 0:nel], nch), sview, ld_pc)
        p.wait("gpsimd", ld_pc, 16 * (j + 1))
        p.wait("gpsimd", st_pc, 16 * j)
        p.op("gpsimd", lambda e, n=nel: e.tensor_copy(pc_bf[:, 0:n], pc_stg[:, 0:n]), cs_pc)
        p.wait("sync", cs_pc, j + 1)
        p.dma("sync", dview, sb3(pc_bf[:, 0:nel], nch), st_pc)

    es4 = ExitStack()
    xr = [es4.enter_context(nc.sbuf_tensor(f"xr4{i}", [128, D], F32)) for i in range(2)]
    ctxs = es4.enter_context(nc.sbuf_tensor("ctxs", [128, 16 * TOWN], BF16))
    wob = es4.enter_context(nc.sbuf_tensor("wob", [128, 16 * 512], BF16))
    attn = es4.enter_context(nc.sbuf_tensor("attn", [128, NTB * D], BF16))
    x1sb = [es4.enter_context(nc.sbuf_tensor(f"x1sb{i}", [128, D], BF16)) for i in range(2)]
    lntm2 = es4.enter_context(nc.sbuf_tensor("lntm2", [128, D], BF16))
    es4o = ExitStack()
    po_ = [es4o.enter_context(nc.psum_tensor(f"po{i}", [128, 512], F32)) for i in range(2)]

    for dc in range(16):
        p.dma("sync", ctxs[:, TOWN * dc : TOWN * dc + TOWN],
              ctx_own[128 * dc : 128 * dc + 128, :], ld_cx)

    p.wait("tensor", v_b, 4)
    for dspan in range(4):
        if dspan > 0:
            p.wait("sync", t_op, t_op.n)
            p.wait("tensor", v_oe, v_oe.n)
        p.wait("sync", st_pc, 16 * NPC_WO)
        p.dma("sync", sb3(wob[:, :], 16), fm3(wo_bfd.ap()[:, 512 * dspan : 512 * dspan + 512]), ld_wo)
        p.wait("tensor", ld_wo, 16 * (dspan + 1))
        if dspan == 0:
            p.wait("tensor", ld_cx, ld_cx.n)
        for tb in range(NTB):
            pb = tb % 2
            if v_oe.n >= 2:
                p.wait("tensor", v_oe, v_oe.n - 1)
            for dc in range(16):
                p.op("tensor", lambda e, dc=dc, tb=tb, pb=pb: e.matmul(
                    po_[pb][:, :], ctxs[:, TOWN * dc + 128 * tb : TOWN * dc + 128 * tb + 128],
                    wob[:, 512 * dc : 512 * dc + 512],
                    start=(dc == 0), stop=(dc == 15)), t_op if dc == 15 else None)
            p.wait("vector", t_op, t_op.n)
            p.op("vector", lambda e, tb=tb, dspan=dspan, pb=pb: e.tensor_add(
                attn[:, D * tb + 512 * dspan : D * tb + 512 * dspan + 512], po_[pb][:, :], bot_r[:, 512 * dspan : 512 * dspan + 512]), v_oe)

    es4o.close()
    es4p = ExitStack()
    ptr2 = [es4p.enter_context(nc.psum_tensor(f"ptr2{i}", [128, 128], BF16)) for i in range(2)]
    for tb in range(NTB):
        b = tb % 2
        if tb >= 2:
            p.wait("sync", v_x1, 2 * (tb - 1))
        p.dma("sync", xr[b][:, :], x_own[128 * tb : 128 * tb + 128, :], ld_x)
        at = attn[:, D * tb : D * tb + D]
        p.wait("vector", v_oe, v_oe.n)
        p.wait("scalar", v_oe, v_oe.n)
        ln_stats(at, D, v_st4, a_rs4)
        p.wait("vector", ld_x, 16 * (tb + 1))
        if tb >= 2:
            p.wait("vector", st_x1, 16 * (tb - 1))
        p.op("vector", lambda e, at=at: e.tensor_scalar(nrmt[:, :], at, stat[:, 3:4], stat[:, 4:5], op0=ALU.mult, op1=ALU.add))
        p.op("vector", lambda e, b=b: e.tensor_add(x1sb[b][:, :], nrmt[:, :], xr[b][:, :]), v_x1)
        p.wait("scalar", v_x1, v_x1.n)
        p.dma("scalar", x1_dram.ap()[128 * tb : 128 * tb + 128, :], x1sb[b][:, :], st_x1)
        p.wait("scalar", v_x1, v_x1.n)
        ln_stats(x1sb[b][:, :], D, v_st4, a_rs4)
        p.op("vector", lambda e, b=b: e.tensor_scalar(lntm2[:, :], x1sb[b][:, :], stat[:, 3:4], stat[:, 4:5], op0=ALU.mult, op1=ALU.add), v_x1)
        p.wait("tensor", v_x1, v_x1.n)
        if tb == 0:
            p.wait("tensor", v_oe, v_oe.n)
        for dc in range(16):
            pb = dc % 2
            if t_tr2.n >= 2:
                p.wait("tensor", v_te2, v_te2.n - 1)
            p.op("tensor", lambda e, dc=dc, pb=pb: e.transpose(ptr2[pb][:, :], lntm2[:, 128 * dc : 128 * dc + 128], ident_sb[:, :]), t_tr2)
            p.wait("vector", t_tr2, t_tr2.n)
            p.op("vector", lambda e, dc=dc, tb=tb, pb=pb: e.tensor_copy(ln2t[:, dc * TOWN + 128 * tb : dc * TOWN + 128 * tb + 128], ptr2[pb][:, :]), v_te2)

    es4p.close()
    es4.close()
    es_pc.close()
    es5 = ExitStack()
    hsb = es5.enter_context(nc.sbuf_tensor("hsb", [128, 64 * 512], BF16))
    gur = [es5.enter_context(nc.sbuf_tensor(f"gur{i}", [128, 16 * 256], BF16)) for i in range(2)]
    wdr = [es5.enter_context(nc.sbuf_tensor(f"wdr{i}", [128, 32 * 128], BF16)) for i in range(2)]
    sil = es5.enter_context(nc.sbuf_tensor("sil", [128, 512], F32))
    ffm = [es5.enter_context(nc.sbuf_tensor(f"ffm{i}", [128, 512], BF16)) for i in range(2)]
    ftm = es5.enter_context(nc.sbuf_tensor("ftm", [128, 4 * D], BF16))
    x1r = [es5.enter_context(nc.sbuf_tensor(f"x1r{i}", [128, D], BF16)) for i in range(2)]
    yt = [es5.enter_context(nc.sbuf_tensor("yt0", [128, D], F32))] * 2
    pg = es5.enter_context(nc.psum_tensor("pg", [128, 512], F32))
    pu = es5.enter_context(nc.psum_tensor("pu", [128, 512], F32))
    pd = [es5.enter_context(nc.psum_tensor(f"pd{i}", [128, 512], F32)) for i in range(2)]
    ptr5 = [es5.enter_context(nc.psum_tensor(f"ptr5{i}", [128, 128], BF16)) for i in range(2)]

    p.wait("sync", v_te2, v_te2.n)
    p.wait("sync", st_pc, 16 * npc)
    p.wait("tensor", v_te2, v_te2.n)
    for span in range(2):
        for ic in range(64):
            b = ic % 2
            if span * 64 + ic >= 2:
                p.wait("sync", t_gu, t_gu.n)
            p.dma("sync", sb3(gur[b][:, :], 16), fm3(gu_bfd.ap()[:, 256 * ic : 256 * ic + 256]), ld_gu)
            p.wait("tensor", ld_gu, 16 * (span * 64 + ic + 1))
            if span > 0 and ic == 0:
                p.wait("vector", t_dn, t_dn.n)
            if span * 64 + ic >= 1:
                p.wait("tensor", v_h, v_h.n)
            for dc in range(16):
                p.op("tensor", lambda e, b=b, dc=dc, span=span: e.matmul(
                    pg[:, :], gur[b][:, 256 * dc : 256 * dc + 128],
                    ln2t[:, dc * TOWN + 512 * span : dc * TOWN + 512 * span + 512],
                    start=(dc == 0), stop=(dc == 15)))
            for dc in range(16):
                p.op("tensor", lambda e, b=b, dc=dc, span=span: e.matmul(
                    pu[:, :], gur[b][:, 256 * dc + 128 : 256 * dc + 256],
                    ln2t[:, dc * TOWN + 512 * span : dc * TOWN + 512 * span + 512],
                    start=(dc == 0), stop=(dc == 15)), t_gu if dc == 15 else None)
            p.wait("scalar", t_gu, t_gu.n)
            p.op("scalar", lambda e: e.activation(sil[:, :], pg[:, :], AF.Silu), a_si)
            p.wait("vector", a_si, a_si.n)
            p.op("vector", lambda e, ic=ic: e.tensor_tensor(hsb[:, 512 * ic : 512 * ic + 512], sil[:, :], pu[:, :], op=ALU.mult), v_h)
        if span > 0:
            p.wait("vector", a_rs5, a_rs5.n)
            p.wait("vector", v_y, v_y.n)
        for dc in range(16):
            db = dc % 2
            if span * 16 + dc >= 1:
                p.wait("tensor", v_de, v_de.n)
            for half in range(2):
                hidx = (span * 16 + dc) * 2 + half
                b = hidx % 2
                if hidx >= 2:
                    p.wait("sync", t_dn, t_dn.n)
                p.dma("sync", sb3(wdr[b][:, :], 32),
                      fm3(wd_bfd.ap()[4096 * half : 4096 * half + 4096, 128 * dc : 128 * dc + 128]), ld_wd)
                p.wait("tensor", ld_wd, 16 * (hidx + 1))
                for ic in range(32):
                    icg = 32 * half + ic
                    p.op("tensor", lambda e, b=b, ic=ic, icg=icg, db=db: e.matmul(
                        pd[db][:, :], wdr[b][:, 128 * ic : 128 * ic + 128],
                        hsb[:, 512 * icg : 512 * icg + 512],
                        start=(icg == 0), stop=(icg == 63)), t_dn if icg == 63 else None)
            p.wait("vector", t_dn, t_dn.n)
            if t_tr5.n >= 8:
                p.wait("vector", t_tr5, t_tr5.n - 4)
            p.op("vector", lambda e, b=(2 * (span * 16 + dc) + 1) % 2, db=dc % 2: e.tensor_copy(ffm[db][:, :], pd[db][:, :]), v_de)
            p.wait("tensor", v_de, v_de.n)
            for j in range(4):
                pb = j % 2
                if t_tr5.n >= 2:
                    p.wait("tensor", v_te5, v_te5.n - 1)
                p.op("tensor", lambda e, j=j, pb=pb, db=dc % 2: e.transpose(ptr5[pb][:, :], ffm[db][:, 128 * j : 128 * j + 128], ident_sb[:, :]), t_tr5)
                p.wait("vector", t_tr5, t_tr5.n)
                p.op("vector", lambda e, j=j, dc=dc, pb=pb: e.tensor_copy(ftm[:, D * j + 128 * dc : D * j + 128 * dc + 128], ptr5[pb][:, :]), v_te5)
        for t4 in range(4):
            tb = 4 * span + t4
            b = t4 % 2
            if tb >= 2:
                p.wait("sync", v_y, v_y.n - 1)
            p.wait("sync", st_x1, 16 * (tb + 1))
            p.dma("sync", x1r[b][:, :], x1_dram.ap()[128 * tb : 128 * tb + 128, :], ld_x1)
            ft = ftm[:, D * t4 : D * t4 + D]
            p.wait("vector", v_te5, v_te5.n)
            p.wait("scalar", v_te5, v_te5.n)
            ln_stats(ft, D, v_st5, a_rs5)
            p.wait("vector", ld_x1, 16 * (tb + 1))
            if tb >= 1:
                p.wait("vector", st_y, 16 * tb)
            p.op("vector", lambda e, ft=ft: e.tensor_scalar(nrmt[:, :], ft, stat[:, 3:4], stat[:, 4:5], op0=ALU.mult, op1=ALU.add))
            p.op("vector", lambda e, b=b: e.tensor_add(yt[b][:, :], nrmt[:, :], x1r[b][:, :]), v_y)
            p.wait("scalar", v_y, v_y.n)
            p.dma("scalar", y[128 * tb : 128 * tb + 128, :], yt[b][:, :], st_y)

    p.wait("sync", st_y, st_y.n)
    return _finish(nc, p, es, es5)


def _finish(nc, p, es, *extra):
    with allow_ici_all_to_all(), nc.Block() as block:
        block.sync(lambda e: [fn(e) for fn in p.ops["sync"]])
        block.scalar(lambda e: [fn(e) for fn in p.ops["scalar"]])
        block.vector(lambda e: [fn(e) for fn in p.ops["vector"]])
        block.tensor(lambda e: [fn(e) for fn in p.ops["tensor"]])
        block.gpsimd(lambda e: [fn(e) for fn in p.ops["gpsimd"]])
    for x in extra:
        x.close()
    es.close()
    return nc


_slopes = (2.0 ** (-8.0 / H)) ** np.arange(1, H + 1, dtype=np.float64)


def _host_inputs(inputs):
    x = np.asarray(inputs["L0_attn_x"]).reshape(S, D)
    Wq = np.asarray(inputs["L0_attn_Wq"]); bq = np.asarray(inputs["L0_attn_bq"])
    Wk = np.asarray(inputs["L0_attn_Wk"]); bk = np.asarray(inputs["L0_attn_bk"])
    Wv = np.asarray(inputs["L0_attn_Wv"]); bv = np.asarray(inputs["L0_attn_bv"])
    Wo = np.asarray(inputs["L0_attn_Wo"]); bo = np.asarray(inputs["L0_attn_bo"])
    Wgu = np.asarray(inputs["L0_ffn_w_gate_up"])
    Wd = np.asarray(inputs["L0_ffn_w_down"])

    perm = []
    for i in range(NC):
        perm += list(range(HD * i, HD * i + HD)) + list(range(HD * (15 - i), HD * (15 - i) + HD))
    wo_perm = np.ascontiguousarray(Wo[perm, :], dtype=np.float32)
    wg = np.ascontiguousarray(Wgu[:INTER].T)
    wu = np.ascontiguousarray(Wgu[INTER:].T)
    w_gu = np.empty((D, 2 * INTER), dtype=np.float32)
    w_gu.reshape(D, 64, 2, 128)[:, :, 0, :] = wg.reshape(D, 64, 128)
    w_gu.reshape(D, 64, 2, 128)[:, :, 1, :] = wu.reshape(D, 64, 128)
    w_dT = np.ascontiguousarray(Wd.T, dtype=np.float32)

    ident = np.eye(128, dtype=ml_dtypes.bfloat16)
    masktri = np.where(np.arange(128)[:, None] < np.arange(128)[None, :],
                       np.float32(MASKVAL), np.float32(0.0)).astype(ml_dtypes.bfloat16)

    in_maps = []
    for r in range(NC):
        h0, h1 = r, 15 - r
        cols = list(range(HD * h0, HD * h0 + HD)) + list(range(HD * h1, HD * h1 + HD))
        al = np.empty((128, 2), dtype=np.float32)
        dl = np.zeros((1088, 1024), dtype=ml_dtypes.bfloat16)
        row = 0
        for hli, h in enumerate((h0, h1)):
            sl = float(_slopes[h])
            al[:, hli] = (sl * np.arange(128)).astype(np.float32)
            for c in range(16):
                for kt in range(4 * (c + 1)):
                    q = 512 * c + np.arange(512)
                    v64 = SQ * (sl * (128 * kt - q) - CSH)
                    hi = v64.astype(ml_dtypes.bfloat16)
                    lo = (v64 - hi.astype(np.float64)).astype(ml_dtypes.bfloat16)
                    dl[row, 0:512] = hi
                    dl[row, 512:1024] = lo
                    row += 1
        assert row == 1088
        in_maps.append({
            "x_own": np.ascontiguousarray(x[TOWN * r : TOWN * r + TOWN, :], dtype=np.float32),
            "wq_own": np.ascontiguousarray(Wq[:, cols], dtype=np.float32),
            "wk_own": np.ascontiguousarray(Wk[:, cols], dtype=np.float32),
            "wv_own": np.ascontiguousarray(Wv[:, cols], dtype=np.float32),
            "bqk": np.ascontiguousarray(np.stack(
                [bq[cols[:128]], bq[cols[128:]], bk[cols[:128]], bk[cols[128:]]], axis=1), dtype=np.float32),
            "bv_own": np.ascontiguousarray(bv[cols][None, :], dtype=np.float32),
            "bo_in": np.ascontiguousarray(bo[None, :], dtype=np.float32),
            "wo_perm": wo_perm,
            "w_gu": w_gu,
            "w_dT": w_dT,
            "alibi": al,
            "delta": dl,
            "masktri": masktri,
            "ident_in": ident,
        })
    return in_maps


_B_KEYS = ("x_own", "bo_in", "wo_perm", "w_gu", "w_dT", "ident_in")


def kernel(**inputs):
    in_maps = _host_inputs(inputs)
    nca = build_program(part="A")
    res_a = _bu.run_bass_kernel_spmd(nca, in_maps, core_ids=list(range(NC)))
    yctx = [np.asarray(res_a.results[i]["yctx"]) for i in range(NC)]
    in_maps_b = []
    for r in range(NC):
        m = {k: in_maps[r][k] for k in _B_KEYS}
        m["ctx_own"] = np.ascontiguousarray(
            np.concatenate([yctx[i][256 * r : 256 * r + 256, :] for i in range(NC)], axis=0))
        in_maps_b.append(m)
    ncb = build_b()
    res_b = _bu.run_bass_kernel_spmd(ncb, in_maps_b, core_ids=list(range(NC)))
    out = np.concatenate([res_b.results[r]["y"] for r in range(NC)], axis=0)
    return out.reshape(1, S, D).astype(np.float32)


# version 05
# speedup vs baseline: 1.0277x; 1.0277x over previous
import numpy as np
import ml_dtypes
import concourse.bass as bass
import concourse.mybir as mybir
import concourse.bass_utils as _bu
from concourse.aot_env import allow_ici_all_to_all
from contextlib import ExitStack

F32 = mybir.dt.float32
BF16 = mybir.dt.bfloat16
AF = mybir.ActivationFunctionType
ALU = mybir.AluOpType
AXX = mybir.AxisListType.X

NC = 8
S = 8192
D = 2048
H = 16
HD = 128
TOWN = S // NC
NTB = TOWN // 128
INTER = 8192
RS = float(1.0 / np.sqrt(HD))
SQ = float(np.sqrt(HD))
CSH = 4.0
EPS = 1e-5
MASKVAL = -1e15

ENGS = ("sync", "scalar", "vector", "tensor", "gpsimd")

_STAGE = 5


class Sem:
    def __init__(self, nc, es, name):
        self.h = es.enter_context(nc.semaphore(name))
        self.n = 0

    def bump(self, k):
        self.n += k
        return self.n


class Prog:
    def __init__(self):
        self.ops = {e: [] for e in ENGS}

    def emit(self, eng, fn):
        self.ops[eng].append(fn)

    def wait(self, eng, sem, val=None):
        v = sem.n if val is None else val
        if v <= 0:
            return
        self.emit(eng, lambda e, s=sem.h, v=v: e.wait_ge(s, v))

    def dma(self, eng, out, in_, sem):
        sem.bump(16)
        self.emit(eng, lambda e, o=out, i=in_, s=sem.h: e.dma_start(out=o, in_=i).then_inc(s, 16))

    def op(self, eng, fn, sem=None, k=1):
        if sem is None:
            self.emit(eng, fn)
        else:
            sem.bump(k)
            self.emit(eng, lambda e, f=fn, s=sem.h, k=k: f(e).then_inc(s, k))


def build_program(stage=None, part="ALL"):
    if stage is None:
        stage = _STAGE
    nc = bass.Bass("TRN2", target_bir_lowering=False, num_devices=NC)
    es = ExitStack()

    def din(name, shape, dt=F32):
        return nc.dram_tensor(name, shape, dt, kind="ExternalInput").ap()

    x_own = din("x_own", [TOWN, D])
    wq_own = din("wq_own", [D, 256])
    wk_own = din("wk_own", [D, 256])
    wv_own = din("wv_own", [D, 256])
    bqk = din("bqk", [128, 4])
    bv_own = din("bv_own", [1, 256])
    bo_in = din("bo_in", [1, D])
    wo_perm = din("wo_perm", [D, D])
    w_gu = din("w_gu", [D, 2 * INTER])
    w_dT = din("w_dT", [INTER, D])
    alibi = din("alibi", [128, 2])
    delta = din("delta", [1088, 1024], BF16)
    masktri = din("masktri", [128, 128], BF16)
    ident_in = din("ident_in", [128, 128], BF16)
    y = nc.dram_tensor("y", [TOWN, D], F32, kind="ExternalOutput").ap()
    dbg = None
    if stage < 5:
        dbg_shape = {1: [2048, TOWN], 2: [640, 16384], 3: [256, S], 4: [128, 16 * TOWN]}[stage]
        dbg = nc.dram_tensor("dbg", dbg_shape, F32, kind="ExternalOutput").ap()

    yctx = nc.dram_tensor("yctx", [NC * 256, TOWN], BF16, kind="ExternalOutput").ap() if part == "A" else None
    ctx_own = din("ctx_own", [NC * 256, TOWN], BF16) if part == "B" else None
    ln1t_in = nc.dram_tensor("ln1t_in", [D, TOWN], BF16, kind="Internal")
    ln1t_out = nc.dram_tensor("ln1t_out", [NC * D, TOWN], BF16, kind="Internal")
    ctx_in = nc.dram_tensor("ctx_in", [NC * 256, TOWN], BF16, kind="Internal")
    ctx_out = nc.dram_tensor("ctx_out", [NC * 256, TOWN], BF16, kind="Internal")
    wo_bfd = nc.dram_tensor("wo_bfd", [D, D], BF16, kind="Internal")
    gu_bfd = nc.dram_tensor("gu_bfd", [D, 2 * INTER], BF16, kind="Internal")
    wd_bfd = nc.dram_tensor("wd_bfd", [INTER, D], BF16, kind="Internal")
    x1_dram = nc.dram_tensor("x1_dram", [TOWN, D], BF16, kind="Internal")

    def sb(name, shape, dt, side=None):
        return es.enter_context(nc.sbuf_tensor(name, shape, dt, side=side))

    alibi_sb = sb("alibi_sb", [128, 2], F32)
    mask_sb = sb("mask_sb", [128, 128], BF16)
    ident_sb = sb("ident_sb", [128, 128], BF16)
    bqk_sb = sb("bqk_sb", [128, 4], F32)
    ones_r = sb("ones_r", [1, 128], BF16)
    ones_c = sb("ones_c", [128, 1], BF16)
    ones_rf = sb("ones_rf", [1, 128], F32)
    delta_rb = sb("delta_rb", [1, 4 * 1024], BF16)
    stat = sb("stat", [128, 8], F32)
    epsv = sb("epsv", [128, 1], F32)
    lrec = sb("lrec", [1, 512], F32)
    sqs = sb("sqs", [128, D], BF16)
    nrmt = sb("nrmt", [128, D], BF16)

    sems = {}

    def S_(name):
        if name not in sems:
            sems[name] = Sem(nc, es, name)
        return sems[name]

    p = Prog()

    def fm3(ap):
        return ap.rearrange("(dc p) c -> p dc c", p=128)

    def sb3(ap, n):
        return ap.rearrange("p (dc c) -> p dc c", dc=n)

    ld_c = S_("ld_c"); ld_w = S_("ld_w"); cs_w = S_("cs_w")
    t_b = S_("t_b"); v_b = S_("v_b"); v_ini = S_("v_ini")
    ld_pc = S_("ld_pc"); cs_pc = S_("cs_pc"); st_pc = S_("st_pc")
    ld_x = S_("ld_x"); v_ln = S_("v_ln"); a_sq = S_("a_sq")
    t_tr = S_("t_tr"); v_te = S_("v_te"); st_l1 = S_("st_l1"); cc = S_("cc")
    v_st = S_("v_st"); a_rs = S_("a_rs")
    ld_g = S_("ld_g"); t_qkv = S_("t_qkv"); v_qe = S_("v_qe")
    s_mm = S_("s_mm"); v_mk = S_("v_mk"); a_ex = S_("a_ex"); s_av = S_("s_av")
    a_rc = S_("a_rc"); t_bc = S_("t_bc"); v_cx = S_("v_cx"); ld_d = S_("ld_d"); st_cx = S_("st_cx")
    ld_cx = S_("ld_cx"); ld_wo = S_("ld_wo"); t_op = S_("t_op"); v_oe = S_("v_oe")
    v_x1 = S_("v_x1"); st_x1 = S_("st_x1"); t_tr2 = S_("t_tr2"); v_te2 = S_("v_te2")
    v_st4 = S_("v_st4"); a_rs4 = S_("a_rs4")
    ld_gu = S_("ld_gu"); ld_wd = S_("ld_wd"); ld_x1 = S_("ld_x1")
    t_gu = S_("t_gu"); a_si = S_("a_si"); v_h = S_("v_h")
    t_dn = S_("t_dn"); v_de = S_("v_de"); t_tr5 = S_("t_tr5"); v_te5 = S_("v_te5")
    v_y = S_("v_y"); st_y = S_("st_y"); v_st5 = S_("v_st5"); a_rs5 = S_("a_rs5")
    st_dbg = S_("st_dbg")
    v_dr = S_("v_dr")

    p.dma("sync", alibi_sb[:, :], alibi, ld_c)
    p.dma("sync", mask_sb[:, :], masktri, ld_c)
    p.dma("sync", ident_sb[:, :], ident_in, ld_c)
    p.dma("sync", bqk_sb[:, :], bqk, ld_c)
    p.op("vector", lambda e: e.memset(ones_r[:, :], 1.0))
    p.op("vector", lambda e: e.memset(ones_c[:, :], 1.0))
    p.op("vector", lambda e: e.memset(ones_rf[:, :], 1.0))
    p.op("vector", lambda e: e.memset(epsv[:, :], float(EPS)), v_ini)

    bot_r = sb("bot_r", [128, D], F32, side="right")
    bvt_r = sb("bvt_r", [128, 256], F32, side="right")
    es_w = ExitStack()
    wqb = es_w.enter_context(nc.sbuf_tensor("wqb", [128, 16 * 256], BF16, side="right"))
    wkb = es_w.enter_context(nc.sbuf_tensor("wkb", [128, 16 * 256], BF16, side="right"))
    wvb = es_w.enter_context(nc.sbuf_tensor("wvb", [128, 16 * 256], BF16, side="right"))

    es_pc = ExitStack()
    pc_stg = es_pc.enter_context(nc.sbuf_tensor("pc_stg", [128, 16 * 128], F32))
    pc_bf = es_pc.enter_context(nc.sbuf_tensor("pc_bf", [128, 16 * 128], BF16))

    with nc.sbuf_tensor("wstg", [128, 16 * 256], F32) as wstg:
        for wi, (src, dst) in enumerate([(wq_own, wqb), (wk_own, wkb), (wv_own, wvb)]):
            src3 = fm3(src)
            if wi > 0:
                p.wait("sync", cs_w, wi)
            p.dma("sync", sb3(wstg[:, :], 16), src3, ld_w)
            p.wait("gpsimd", ld_w, 16 * (wi + 1))
            p.op("gpsimd", lambda e, d=dst: e.tensor_copy(d[:, :], wstg[:, :]), cs_w)

        with (
            nc.sbuf_tensor("bv_f", [1, 256], F32) as bv_f,
            nc.sbuf_tensor("bo_f", [1, D], F32) as bo_f,
            nc.psum_tensor("pb0", [128, 512], F32) as pb0,
            nc.psum_tensor("pb1", [128, 512], F32) as pb1,
        ):
            p.dma("sync", bv_f[:, :], bv_own, ld_c)
            p.dma("sync", bo_f[:, :], bo_in, ld_c)
            p.wait("tensor", ld_c, ld_c.n)
            p.wait("tensor", v_ini, 1)
            p.op("tensor", lambda e: e.matmul(pb0[:, 0:256], ones_rf[:, :], bv_f[:, :], start=True, stop=True), t_b)
            p.wait("vector", t_b, 1)
            p.op("vector", lambda e: e.tensor_copy(bvt_r[:, :], pb0[:, 0:256]), v_b)
            for j in range(4):
                p.op("tensor", lambda e, j=j: e.matmul(pb1[:, :], ones_rf[:, :], bo_f[:, 512 * j : 512 * j + 512], start=True, stop=True), t_b)
                p.wait("vector", t_b, 2 + j)
                p.op("vector", lambda e, j=j: e.tensor_copy(bot_r[:, 512 * j : 512 * j + 512], pb1[:, :]), v_b)
                if j < 3:
                    p.wait("tensor", v_b, 2 + j)

    pcast_jobs = []

    def plan_precast(src, dst, rows, cols, col_blk):
        for c0 in range(0, cols, col_blk):
            w = min(col_blk, cols - c0)
            sview = fm3(src[:, c0 : c0 + w])
            dview = fm3(dst.ap()[:, c0 : c0 + w])
            pcast_jobs.append((sview, dview, (rows // 128) * w, rows // 128))

    plan_precast(wo_perm, wo_bfd, D, D, 128)
    plan_precast(w_gu, gu_bfd, D, 2 * INTER, 128)
    plan_precast(w_dT, wd_bfd, INTER, D, 32)
    NPC_WO, NPC_GU, NPC_WD = 16, 128, 64
    npc = len(pcast_jobs)

    def emit_precast(n):
        for _ in range(n):
            if not pcast_jobs:
                return
            sview, dview, nel, nch = pcast_jobs.pop(0)
            j = emit_precast.j
            emit_precast.j += 1
            p.wait("sync", cs_pc, j)
            p.dma("sync", sb3(pc_stg[:, 0:nel], nch), sview, ld_pc)
            p.wait("gpsimd", ld_pc, 16 * (j + 1))
            p.wait("gpsimd", st_pc, 16 * j)
            p.op("gpsimd", lambda e, n=nel: e.tensor_copy(pc_bf[:, 0:n], pc_stg[:, 0:n]), cs_pc)
            p.wait("sync", cs_pc, j + 1)
            p.dma("sync", dview, sb3(pc_bf[:, 0:nel], nch), st_pc)
    emit_precast.j = 0

    es1 = ExitStack()
    xr = [es1.enter_context(nc.sbuf_tensor(f"xr{i}", [128, D], F32)) for i in range(2)]
    lntm = [es1.enter_context(nc.sbuf_tensor(f"lntm{i}", [128, D], BF16)) for i in range(2)]
    ln1t = es1.enter_context(nc.sbuf_tensor("ln1t", [128, 16 * TOWN], BF16))
    ptr1 = [es1.enter_context(nc.psum_tensor(f"ptr1{i}", [128, 128], BF16)) for i in range(2)]

    def ln_stats(src_tile, d_free, v_done, a_done):
        p.op("vector", lambda e, s=src_tile: e.reduce_sum(stat[:, 0:1], s, axis=AXX), v_dr)
        p.wait("vector", v_dr, v_dr.n)
        p.op("vector", lambda e, d=d_free: e.tensor_scalar_mul(stat[:, 1:2], stat[:, 0:1], 1.0 / d), v_dr)
        p.op("scalar", lambda e, s=src_tile, d=d_free: e.activation(sqs[:, 0:d], s, AF.Square, accum_out=stat[:, 5:6]), a_sq)
        p.wait("vector", a_sq, a_sq.n)
        p.wait("vector", v_dr, v_dr.n)
        p.op("vector", lambda e: e.tensor_scalar(stat[:, 7:8], stat[:, 0:1], stat[:, 1:2], None, op0=ALU.mult), v_dr)
        p.wait("vector", v_dr, v_dr.n)
        p.op("vector", lambda e: e.tensor_sub(stat[:, 2:3], stat[:, 7:8], stat[:, 5:6]), v_done)
        p.wait("scalar", v_done, v_done.n)
        p.op("scalar", lambda e, d=d_free: e.activation(stat[:, 6:7], stat[:, 2:3], AF.Sqrt, bias=epsv[:, 0:1], scale=-1.0 / d), a_done)
        p.wait("vector", a_done, a_done.n)
        p.op("vector", lambda e: e.reciprocal(stat[:, 3:4], stat[:, 6:7]), v_dr)
        p.wait("vector", v_dr, v_dr.n)
        p.op("vector", lambda e: e.tensor_scalar(stat[:, 4:5], stat[:, 1:2], stat[:, 3:4], -1.0, op0=ALU.mult, op1=ALU.mult), v_done)
        p.wait("vector", v_done, v_done.n)

    p.wait("vector", v_b, 5)
    p.wait("tensor", v_b, 5)
    p.wait("vector", cs_w, 3)
    p.wait("sync", cs_w, 3)
    p.wait("sync", t_b, 5)

    for tb in range(NTB):
        b = tb % 2
        if tb >= 2:
            p.wait("sync", v_ln, tb - 1)
        p.dma("sync", xr[b][:, :], x_own[128 * tb : 128 * tb + 128, :], ld_x)
        p.wait("vector", ld_x, 16 * (tb + 1))
        p.wait("scalar", ld_x, 16 * (tb + 1))
        ln_stats(xr[b][:, :], D, v_st, a_rs)
        if tb >= 2:
            p.wait("vector", t_tr, 16 * (tb - 1))
        p.op("vector", lambda e, b=b: e.tensor_scalar(lntm[b][:, :], xr[b][:, :], stat[:, 3:4], stat[:, 4:5], op0=ALU.mult, op1=ALU.add), v_ln)
        p.wait("tensor", v_ln, v_ln.n)
        for dc in range(16):
            pb = dc % 2
            if t_tr.n >= 2:
                p.wait("tensor", v_te, v_te.n - 1)
            p.op("tensor", lambda e, b=b, dc=dc, pb=pb: e.transpose(ptr1[pb][:, :], lntm[b][:, 128 * dc : 128 * dc + 128], ident_sb[:, :]), t_tr)
            p.wait("vector", t_tr, t_tr.n)
            p.op("vector", lambda e, dc=dc, tb=tb, pb=pb: e.tensor_copy(ln1t[:, dc * TOWN + 128 * tb : dc * TOWN + 128 * tb + 128], ptr1[pb][:, :]), v_te)
    p.wait("scalar", v_te, v_te.n)
    for dc in range(16):
        p.dma("scalar", ln1t_in.ap()[128 * dc : 128 * dc + 128, :], ln1t[:, dc * TOWN : (dc + 1) * TOWN], st_l1)
    p.wait("gpsimd", st_l1, st_l1.n)
    p.op("gpsimd", lambda e: e.collective_compute(
        "AllGather", ALU.bypass, replica_groups=[list(range(NC))],
        ins=[ln1t_in.ap().opt()], outs=[ln1t_out.ap().opt()]), cc)

    emit_precast(32)

    if stage == 1:
        with nc.sbuf_tensor("dbf", [128, TOWN], F32) as dbf:
            vdb = S_("vdb")
            for dc in range(16):
                p.wait("vector", st_dbg, 16 * dc)
                p.op("vector", lambda e, dc=dc: e.tensor_copy(dbf[:, :], ln1t[:, dc * TOWN : (dc + 1) * TOWN]), vdb)
                p.wait("scalar", vdb, vdb.n)
                p.dma("scalar", dbg[128 * dc : 128 * dc + 128, 0:TOWN], dbf[:, :], st_dbg)
        p.wait("sync", st_dbg, st_dbg.n)
        return _finish(nc, p, es, es1, es_pc, es_w)

    es1.close()
    es23 = ExitStack()
    qsb = [es23.enter_context(nc.sbuf_tensor(f"qsb{hl}", [128, S], BF16)) for hl in range(2)]
    ksb = [es23.enter_context(nc.sbuf_tensor(f"ksb{hl}", [128, S], BF16)) for hl in range(2)]
    vsb = es23.enter_context(nc.sbuf_tensor("vsb", [128, 64 * 256], BF16))
    es2 = ExitStack()
    lng = [es2.enter_context(nc.sbuf_tensor(f"lng{i}", [128, 16 * 512], BF16)) for i in range(2)]
    pq = [es2.enter_context(nc.psum_tensor(f"pq{i}", [128, 512], F32)) for i in range(2)]
    pv = [es2.enter_context(nc.psum_tensor(f"pv{i}", [128, 512], F32)) for i in range(2)]

    p.wait("sync", cc, 1)
    p.wait("tensor", cs_w, 3)
    p.wait("tensor", v_te, v_te.n)
    for g in range(16):
        b = g % 2
        rank, half = g // 2, g % 2
        if g >= 2:
            p.wait("sync", t_qkv, t_qkv.n)
        src = ln1t_out.ap()[D * rank : D * rank + D, 512 * half : 512 * half + 512]
        p.dma("sync", sb3(lng[b][:, :], 16), fm3(src), ld_g)
        p.wait("tensor", ld_g, 16 * (g + 1))
        for which, (wsb, qk, bcol0) in enumerate([(wqb, qsb, 0), (wkb, ksb, 2)]):
            for hl in range(2):
                pb = (which * 2 + hl) % 2
                if v_qe.n >= 2:
                    p.wait("tensor", v_qe, v_qe.n - 1)
                for dc in range(16):
                    p.op("tensor", lambda e, w=wsb, dc=dc, hl=hl, b=b, pb=pb: e.matmul(
                        pq[pb][:, :], w[:, dc * 256 + hl * 128 : dc * 256 + hl * 128 + 128],
                        lng[b][:, dc * 512 : dc * 512 + 512],
                        start=(dc == 0), stop=(dc == 15)), t_qkv if dc == 15 else None)
                p.wait("vector", t_qkv, t_qkv.n)
                p.op("vector", lambda e, qk=qk, hl=hl, g=g, pb=pb, bcol=bcol0 + hl: e.tensor_scalar(
                    qk[hl][:, 512 * g : 512 * g + 512], pq[pb][:, :], bqk_sb[:, bcol : bcol + 1], None, op0=ALU.add), v_qe)
        for t4 in range(4):
            pb = t4 % 2
            if v_qe.n >= 2:
                p.wait("tensor", v_qe, v_qe.n - 1)
            for dc in range(16):
                p.op("tensor", lambda e, b=b, dc=dc, t4=t4, pb=pb: e.matmul(
                    pv[pb][:, 0:256], lng[b][:, dc * 512 + 128 * t4 : dc * 512 + 128 * t4 + 128],
                    wvb[:, dc * 256 : dc * 256 + 256],
                    start=(dc == 0), stop=(dc == 15)), t_qkv if dc == 15 else None)
            p.wait("vector", t_qkv, t_qkv.n)
            tok = 512 * g + 128 * t4
            p.op("vector", lambda e, tok=tok, pb=pb: e.tensor_add(
                vsb[:, tok * 2 : tok * 2 + 256], pv[pb][:, 0:256], bvt_r[:, :]), v_qe)
        if g == 7:
            emit_precast(32)

    emit_precast(32)

    if stage == 2:
        with nc.sbuf_tensor("dbf", [128, 2048], F32) as dbf:
            vdb = S_("vdb")
            p.wait("vector", v_qe, v_qe.n)
            for i, (t, w) in enumerate([(qsb[0], S), (qsb[1], S), (ksb[0], S), (ksb[1], S), (vsb, 16384)]):
                for cchunk in range(0, w, 2048):
                    p.wait("vector", st_dbg, st_dbg.n)
                    p.op("vector", lambda e, t=t, c=cchunk: e.tensor_copy(dbf[:, :], t[:, c : c + 2048]), vdb)
                    p.wait("scalar", vdb, vdb.n)
                    p.dma("scalar", dbg[128 * i : 128 * i + 128, cchunk : cchunk + 2048], dbf[:, :], st_dbg)
        p.wait("sync", st_dbg, st_dbg.n)
        return _finish(nc, p, es, es2, es23, es_pc, es_w)

    es_w.close()
    es2.close()
    es3 = ExitStack()
    pp = [es3.enter_context(nc.sbuf_tensor(f"pp{i}", [128, 512], BF16)) for i in range(2)]
    cxe = [es3.enter_context(nc.sbuf_tensor(f"cxe{i}", [128, 512], BF16)) for i in range(2)]
    lbc = es3.enter_context(nc.sbuf_tensor("lbc", [128, 512], F32))
    ps_ = [es3.enter_context(nc.psum_tensor(f"ps{i}", [128, 512], F32)) for i in range(2)]
    pctx = [es3.enter_context(nc.psum_tensor(f"pctx{i}", [128, 512], F32)) for i in range(2)]
    pl_ = [es3.enter_context(nc.psum_tensor(f"pl{i}", [1, 512], F32)) for i in range(2)]
    pbc = es3.enter_context(nc.psum_tensor("pbc", [128, 512], F32))

    p.wait("tensor", v_qe, v_qe.n)
    kt_g = 0
    ch_g = 0
    exp_of = {}
    av_of = {}
    smm_of = {}
    drow = 0
    for hl in range(2):
        for c in range(16):
            cb = ch_g % 2
            nkt = 4 * (c + 1)
            if ch_g >= 1:
                p.wait("tensor", v_cx, ch_g)
            for kt in range(nkt):
                sb_ = kt_g % 2
                dslot = kt_g % 4
                j = kt - 4 * c
                col0 = 128 * j if j >= 0 else 0
                if kt_g >= 4:
                    p.wait("sync", s_mm, smm_of[kt_g - 4])
                p.dma("sync", delta_rb[0:1, 1024 * dslot : 1024 * dslot + 1024], delta[drow : drow + 1, :], ld_d)
                drow += 1
                if kt_g >= 2:
                    p.wait("tensor", a_ex, exp_of[kt_g - 2])
                p.wait("tensor", ld_d, 16 * kt_g + 16)
                p.op("tensor", lambda e, hl=hl, kt=kt, c=c, sb_=sb_: e.matmul(
                    ps_[sb_][:, :], ksb[hl][:, 128 * kt : 128 * kt + 128],
                    qsb[hl][:, 512 * c : 512 * c + 512], start=True, stop=False))
                p.op("tensor", lambda e, sb_=sb_, ds=dslot: e.matmul(
                    ps_[sb_][:, :], ones_r[:, :], delta_rb[0:1, 1024 * ds : 1024 * ds + 512],
                    start=False, stop=False))
                p.op("tensor", lambda e, sb_=sb_, ds=dslot, sp=(j < 0): e.matmul(
                    ps_[sb_][:, :], ones_r[:, :], delta_rb[0:1, 1024 * ds + 512 : 1024 * ds + 1024],
                    start=False, stop=(j < 0)), s_mm if j < 0 else None)
                if j >= 0:
                    p.op("tensor", lambda e, sb_=sb_, col0=col0: e.matmul(
                        ps_[sb_][:, col0 : col0 + 128], mask_sb[:, :], ident_sb[:, :],
                        start=False, stop=True), s_mm)
                smm_of[kt_g] = s_mm.n
                p.wait("scalar", s_mm, s_mm.n)
                if kt_g >= 2:
                    p.wait("scalar", s_av, av_of[kt_g - 2])
                p.op("scalar", lambda e, sb_=sb_, col0=col0, hl=hl: e.activation(
                    pp[sb_][:, col0:512], ps_[sb_][:, col0:512], AF.Exp,
                    bias=alibi_sb[:, hl : hl + 1], scale=RS), a_ex)
                exp_of[kt_g] = a_ex.n
                p.wait("tensor", a_ex, a_ex.n)
                p.op("tensor", lambda e, kt=kt, hl=hl, cb=cb, sb_=sb_, col0=col0, st=(kt == 0), sp=(kt == nkt - 1): e.matmul(
                    pctx[cb][:, col0:512], vsb[:, 256 * kt + 128 * hl : 256 * kt + 128 * hl + 128],
                    pp[sb_][:, col0:512], start=st, stop=sp))
                p.op("tensor", lambda e, cb=cb, sb_=sb_, col0=col0, st=(kt == 0), sp=(kt == nkt - 1): e.matmul(
                    pl_[cb][0:1, col0:512], ones_c[:, :], pp[sb_][:, col0:512], start=st, stop=sp), s_av)
                av_of[kt_g] = s_av.n
                kt_g += 1
            p.wait("vector", s_av, s_av.n)
            if ch_g >= 1:
                p.wait("vector", t_bc, t_bc.n)
            p.op("vector", lambda e, cb=cb: e.reciprocal(lrec[0:1, :], pl_[cb][0:1, :]), a_rc)
            if ch_g >= 1:
                p.wait("tensor", v_cx, ch_g)
            p.wait("tensor", a_rc, a_rc.n)
            p.op("tensor", lambda e: e.matmul(pbc[:, :], ones_rf[:, :], lrec[0:1, :], start=True, stop=True), t_bc)
            p.wait("vector", t_bc, t_bc.n)
            if ch_g >= 2:
                p.wait("vector", st_cx, 16 * (ch_g - 1))
            p.op("vector", lambda e: e.tensor_copy(lbc[:, :], pbc[:, :]))
            p.op("vector", lambda e, cb=cb: e.tensor_tensor(cxe[cb][:, :], pctx[cb][:, :], lbc[:, :], op=ALU.mult), v_cx)
            p.wait("scalar", v_cx, v_cx.n)
            ctx_dst = yctx if part == "A" else ctx_in.ap()
            p.dma("scalar", ctx_dst[256 * (c // 2) + 128 * hl : 256 * (c // 2) + 128 * hl + 128,
                                    512 * (c % 2) : 512 * (c % 2) + 512], cxe[cb][:, :], st_cx)
            ch_g += 1
            if ch_g == 16:
                emit_precast(npc)
    if part == "A":
        p.wait("sync", st_cx, st_cx.n)
        return _finish(nc, p, es, es3, es23, es_pc)
    p.wait("gpsimd", st_cx, st_cx.n)
    p.op("gpsimd", lambda e: e.collective_compute(
        "AllToAll", ALU.bypass, replica_groups=[list(range(NC))],
        ins=[ctx_in.ap().opt()], outs=[ctx_out.ap().opt()]), cc)

    if stage == 3:
        p.wait("sync", st_cx, st_cx.n)
        with nc.sbuf_tensor("dbf", [128, TOWN], F32) as dbf, nc.sbuf_tensor("dbb", [128, TOWN], BF16) as dbb:
            vdb = S_("vdb")
            lddb = S_("lddb")
            for i in range(2):
                for j in range(NC):
                    k = i * NC + j
                    if k > 0:
                        p.wait("sync", S_("vdb"), k)
                    p.dma("sync", dbb[:, :], ctx_in.ap()[256 * j + 128 * i : 256 * j + 128 * i + 128, :], lddb)
                    p.wait("vector", lddb, 16 * (k + 1))
                    p.wait("vector", st_dbg, 16 * k)
                    p.op("vector", lambda e: e.tensor_copy(dbf[:, :], dbb[:, :]), vdb)
                    p.wait("scalar", vdb, vdb.n)
                    p.dma("scalar", dbg[128 * i : 128 * i + 128, TOWN * j : TOWN * j + TOWN], dbf[:, :], st_dbg)
        p.wait("sync", st_dbg, st_dbg.n)
        return _finish(nc, p, es, es3, es23, es_pc)

    es3.close()
    es23.close()
    es_pc.close()
    ln2t = sb("ln2t", [128, 16 * TOWN], BF16, side="right")
    es4 = ExitStack()
    xr = [es4.enter_context(nc.sbuf_tensor(f"xr4{i}", [128, D], F32)) for i in range(2)]
    ctxs = es4.enter_context(nc.sbuf_tensor("ctxs", [128, 16 * TOWN], BF16))
    wob = es4.enter_context(nc.sbuf_tensor("wob", [128, 16 * 512], BF16))
    attn = es4.enter_context(nc.sbuf_tensor("attn", [128, NTB * D], BF16))
    x1sb = [es4.enter_context(nc.sbuf_tensor(f"x1sb{i}", [128, D], BF16)) for i in range(2)]
    lntm2 = es4.enter_context(nc.sbuf_tensor("lntm2", [128, D], BF16))
    es4o = ExitStack()
    po_ = [es4o.enter_context(nc.psum_tensor(f"po{i}", [128, 512], F32)) for i in range(2)]

    p.wait("sync", cc, 2)
    p.wait("sync", st_pc, 16 * npc)

    for dc in range(16):
        p.dma("sync", ctxs[:, TOWN * dc : TOWN * dc + TOWN],
              ctx_out.ap()[128 * dc : 128 * dc + 128, :], ld_cx)

    p.wait("tensor", v_cx, 32)
    for dspan in range(4):
        if dspan > 0:
            p.wait("sync", t_op, t_op.n)
            p.wait("tensor", v_oe, v_oe.n)
        p.dma("sync", sb3(wob[:, :], 16), fm3(wo_bfd.ap()[:, 512 * dspan : 512 * dspan + 512]), ld_wo)
        p.wait("tensor", ld_wo, 16 * (dspan + 1))
        if dspan == 0:
            p.wait("tensor", ld_cx, ld_cx.n)
        for tb in range(NTB):
            pb = tb % 2
            if v_oe.n >= 2:
                p.wait("tensor", v_oe, v_oe.n - 1)
            for dc in range(16):
                p.op("tensor", lambda e, dc=dc, tb=tb, pb=pb: e.matmul(
                    po_[pb][:, :], ctxs[:, TOWN * dc + 128 * tb : TOWN * dc + 128 * tb + 128],
                    wob[:, 512 * dc : 512 * dc + 512],
                    start=(dc == 0), stop=(dc == 15)), t_op if dc == 15 else None)
            p.wait("vector", t_op, t_op.n)
            p.op("vector", lambda e, tb=tb, dspan=dspan, pb=pb: e.tensor_add(
                attn[:, D * tb + 512 * dspan : D * tb + 512 * dspan + 512], po_[pb][:, :], bot_r[:, 512 * dspan : 512 * dspan + 512]), v_oe)

    p.wait("sync", v_ln, 8)
    es4o.close()
    es4p = ExitStack()
    ptr2 = [es4p.enter_context(nc.psum_tensor(f"ptr2{i}", [128, 128], BF16)) for i in range(2)]
    for tb in range(NTB):
        b = tb % 2
        if tb >= 2:
            p.wait("sync", v_x1, 2 * (tb - 1))
        p.dma("sync", xr[b][:, :], x_own[128 * tb : 128 * tb + 128, :], ld_x)
        at = attn[:, D * tb : D * tb + D]
        p.wait("vector", v_oe, v_oe.n)
        p.wait("scalar", v_oe, v_oe.n)
        ln_stats(at, D, v_st4, a_rs4)
        p.wait("vector", ld_x, 16 * (NTB + tb + 1))
        if tb >= 2:
            p.wait("vector", st_x1, 16 * (tb - 1))
        p.op("vector", lambda e, at=at: e.tensor_scalar(nrmt[:, :], at, stat[:, 3:4], stat[:, 4:5], op0=ALU.mult, op1=ALU.add), v_dr)
        p.wait("vector", v_dr, v_dr.n)
        p.op("vector", lambda e, b=b: e.tensor_add(x1sb[b][:, :], nrmt[:, :], xr[b][:, :]), v_x1)
        p.wait("scalar", v_x1, v_x1.n)
        p.dma("scalar", x1_dram.ap()[128 * tb : 128 * tb + 128, :], x1sb[b][:, :], st_x1)
        p.wait("scalar", v_x1, v_x1.n)
        ln_stats(x1sb[b][:, :], D, v_st4, a_rs4)
        p.op("vector", lambda e, b=b: e.tensor_scalar(lntm2[:, :], x1sb[b][:, :], stat[:, 3:4], stat[:, 4:5], op0=ALU.mult, op1=ALU.add), v_x1)
        p.wait("tensor", v_x1, v_x1.n)
        if tb == 0:
            p.wait("tensor", v_oe, v_oe.n)
        for dc in range(16):
            pb = dc % 2
            if t_tr2.n >= 2:
                p.wait("tensor", v_te2, v_te2.n - 1)
            p.op("tensor", lambda e, dc=dc, pb=pb: e.transpose(ptr2[pb][:, :], lntm2[:, 128 * dc : 128 * dc + 128], ident_sb[:, :]), t_tr2)
            p.wait("vector", t_tr2, t_tr2.n)
            p.op("vector", lambda e, dc=dc, tb=tb, pb=pb: e.tensor_copy(ln2t[:, dc * TOWN + 128 * tb : dc * TOWN + 128 * tb + 128], ptr2[pb][:, :]), v_te2)

    if stage == 4:
        with nc.sbuf_tensor("dbf", [128, 2048], F32) as dbf:
            vdb = S_("vdb")
            p.wait("vector", v_te2, v_te2.n)
            for cc_ in range(0, 16 * TOWN, 2048):
                p.wait("vector", st_dbg, st_dbg.n)
                p.op("vector", lambda e, c=cc_: e.tensor_copy(dbf[:, :], ln2t[:, c : c + 2048]), vdb)
                p.wait("scalar", vdb, vdb.n)
                p.dma("scalar", dbg[0:128, cc_ : cc_ + 2048], dbf[:, :], st_dbg)
        p.wait("sync", st_dbg, st_dbg.n)
        return _finish(nc, p, es, es4p, es4)

    es4p.close()
    es4.close()
    es5 = ExitStack()
    hsb = es5.enter_context(nc.sbuf_tensor("hsb", [128, 64 * 512], BF16))
    gur = [es5.enter_context(nc.sbuf_tensor(f"gur{i}", [128, 16 * 256], BF16)) for i in range(2)]
    wdr = [es5.enter_context(nc.sbuf_tensor(f"wdr{i}", [128, 32 * 128], BF16)) for i in range(2)]
    sil = es5.enter_context(nc.sbuf_tensor("sil", [128, 512], F32))
    ffm = [es5.enter_context(nc.sbuf_tensor(f"ffm{i}", [128, 512], BF16)) for i in range(2)]
    ftm = es5.enter_context(nc.sbuf_tensor("ftm", [128, 4 * D], BF16))
    x1r = [es5.enter_context(nc.sbuf_tensor(f"x1r{i}", [128, D], BF16)) for i in range(2)]
    yt = [es5.enter_context(nc.sbuf_tensor("yt0", [128, D], F32))] * 2
    pg = es5.enter_context(nc.psum_tensor("pg", [128, 512], F32))
    pu = es5.enter_context(nc.psum_tensor("pu", [128, 512], F32))
    pd = [es5.enter_context(nc.psum_tensor(f"pd{i}", [128, 512], F32)) for i in range(2)]
    ptr5 = [es5.enter_context(nc.psum_tensor(f"ptr5{i}", [128, 128], BF16)) for i in range(2)]

    p.wait("sync", v_te2, v_te2.n)
    p.wait("tensor", v_te2, v_te2.n)
    for span in range(2):
        for ic in range(64):
            b = ic % 2
            if span * 64 + ic >= 2:
                p.wait("sync", t_gu, t_gu.n)
            p.dma("sync", sb3(gur[b][:, :], 16), fm3(gu_bfd.ap()[:, 256 * ic : 256 * ic + 256]), ld_gu)
            p.wait("tensor", ld_gu, 16 * (span * 64 + ic + 1))
            if span > 0 and ic == 0:
                p.wait("vector", t_dn, t_dn.n)
            if span * 64 + ic >= 1:
                p.wait("tensor", v_h, v_h.n)
            for dc in range(16):
                p.op("tensor", lambda e, b=b, dc=dc, span=span: e.matmul(
                    pg[:, :], gur[b][:, 256 * dc : 256 * dc + 128],
                    ln2t[:, dc * TOWN + 512 * span : dc * TOWN + 512 * span + 512],
                    start=(dc == 0), stop=(dc == 15)))
            for dc in range(16):
                p.op("tensor", lambda e, b=b, dc=dc, span=span: e.matmul(
                    pu[:, :], gur[b][:, 256 * dc + 128 : 256 * dc + 256],
                    ln2t[:, dc * TOWN + 512 * span : dc * TOWN + 512 * span + 512],
                    start=(dc == 0), stop=(dc == 15)), t_gu if dc == 15 else None)
            p.wait("scalar", t_gu, t_gu.n)
            p.op("scalar", lambda e: e.activation(sil[:, :], pg[:, :], AF.Silu), a_si)
            p.wait("vector", a_si, a_si.n)
            p.op("vector", lambda e, ic=ic: e.tensor_tensor(hsb[:, 512 * ic : 512 * ic + 512], sil[:, :], pu[:, :], op=ALU.mult), v_h)
        if span > 0:
            p.wait("vector", a_rs5, a_rs5.n)
            p.wait("vector", v_y, v_y.n)
        for dc in range(16):
            db = dc % 2
            if span * 16 + dc >= 1:
                p.wait("tensor", v_de, v_de.n)
            for half in range(2):
                hidx = (span * 16 + dc) * 2 + half
                b = hidx % 2
                if hidx >= 2:
                    p.wait("sync", t_dn, t_dn.n)
                p.dma("sync", sb3(wdr[b][:, :], 32),
                      fm3(wd_bfd.ap()[4096 * half : 4096 * half + 4096, 128 * dc : 128 * dc + 128]), ld_wd)
                p.wait("tensor", ld_wd, 16 * (hidx + 1))
                for ic in range(32):
                    icg = 32 * half + ic
                    p.op("tensor", lambda e, b=b, ic=ic, icg=icg, db=db: e.matmul(
                        pd[db][:, :], wdr[b][:, 128 * ic : 128 * ic + 128],
                        hsb[:, 512 * icg : 512 * icg + 512],
                        start=(icg == 0), stop=(icg == 63)), t_dn if icg == 63 else None)
            p.wait("vector", t_dn, t_dn.n)
            if t_tr5.n >= 8:
                p.wait("vector", t_tr5, t_tr5.n - 4)
            p.op("vector", lambda e, b=b, db=dc % 2: e.tensor_copy(ffm[b][:, :], pd[db][:, :]), v_de)
            p.wait("tensor", v_de, v_de.n)
            for j in range(4):
                pb = j % 2
                if t_tr5.n >= 2:
                    p.wait("tensor", v_te5, v_te5.n - 1)
                p.op("tensor", lambda e, b=b, j=j, pb=pb: e.transpose(ptr5[pb][:, :], ffm[b][:, 128 * j : 128 * j + 128], ident_sb[:, :]), t_tr5)
                p.wait("vector", t_tr5, t_tr5.n)
                p.op("vector", lambda e, j=j, dc=dc, pb=pb: e.tensor_copy(ftm[:, D * j + 128 * dc : D * j + 128 * dc + 128], ptr5[pb][:, :]), v_te5)
        for t4 in range(4):
            tb = 4 * span + t4
            b = t4 % 2
            if tb >= 2:
                p.wait("sync", v_y, v_y.n - 1)
            p.wait("sync", st_x1, 16 * (tb + 1))
            p.dma("sync", x1r[b][:, :], x1_dram.ap()[128 * tb : 128 * tb + 128, :], ld_x1)
            ft = ftm[:, D * t4 : D * t4 + D]
            p.wait("vector", v_te5, v_te5.n)
            p.wait("scalar", v_te5, v_te5.n)
            ln_stats(ft, D, v_st5, a_rs5)
            p.wait("vector", ld_x1, 16 * (tb + 1))
            if tb >= 1:
                p.wait("vector", st_y, 16 * tb)
            p.op("vector", lambda e, ft=ft: e.tensor_scalar(nrmt[:, :], ft, stat[:, 3:4], stat[:, 4:5], op0=ALU.mult, op1=ALU.add), v_dr)
            p.wait("vector", v_dr, v_dr.n)
            p.op("vector", lambda e, b=b: e.tensor_add(yt[b][:, :], nrmt[:, :], x1r[b][:, :]), v_y)
            p.wait("scalar", v_y, v_y.n)
            p.dma("scalar", y[128 * tb : 128 * tb + 128, :], yt[b][:, :], st_y)

    p.wait("sync", st_y, st_y.n)
    return _finish(nc, p, es, es5)


def build_b():
    nc = bass.Bass("TRN2", target_bir_lowering=False, num_devices=NC)
    es = ExitStack()

    def din(name, shape, dt=F32):
        return nc.dram_tensor(name, shape, dt, kind="ExternalInput").ap()

    x_own = din("x_own", [TOWN, D])
    ctx_own = din("ctx_own", [NC * 256, TOWN], BF16)
    bo_in = din("bo_in", [1, D])
    wo_perm = din("wo_perm", [D, D])
    w_gu = din("w_gu", [D, 2 * INTER])
    w_dT = din("w_dT", [INTER, D])
    ident_in = din("ident_in", [128, 128], BF16)
    y = nc.dram_tensor("y", [TOWN, D], F32, kind="ExternalOutput").ap()

    wo_bfd = nc.dram_tensor("wo_bfd", [D, D], BF16, kind="Internal")
    gu_bfd = nc.dram_tensor("gu_bfd", [D, 2 * INTER], BF16, kind="Internal")
    wd_bfd = nc.dram_tensor("wd_bfd", [INTER, D], BF16, kind="Internal")
    x1_dram = nc.dram_tensor("x1_dram", [TOWN, D], BF16, kind="Internal")

    def sb(name, shape, dt, side=None):
        return es.enter_context(nc.sbuf_tensor(name, shape, dt, side=side))

    ident_sb = sb("ident_sb", [128, 128], BF16)
    ones_rf = sb("ones_rf", [1, 128], F32)
    stat = sb("stat", [128, 8], F32)
    epsv = sb("epsv", [128, 1], F32)
    sqs = sb("sqs", [128, D], BF16)
    nrmt = sb("nrmt", [128, D], BF16)
    bot_r = sb("bot_r", [128, D], F32, side="right")
    ln2t = sb("ln2t", [128, 16 * TOWN], BF16, side="right")

    sems = {}

    def S_(name):
        if name not in sems:
            sems[name] = Sem(nc, es, name)
        return sems[name]

    p = Prog()

    def fm3(ap):
        return ap.rearrange("(dc p) c -> p dc c", p=128)

    def sb3(ap, n):
        return ap.rearrange("p (dc c) -> p dc c", dc=n)

    ld_c = S_("ld_c"); v_ini = S_("v_ini"); t_b = S_("t_b"); v_b = S_("v_b")
    ld_pc = S_("ld_pc"); cs_pc = S_("cs_pc"); st_pc = S_("st_pc")
    ld_x = S_("ld_x"); a_sq = S_("a_sq"); v_dr = S_("v_dr")
    ld_cx = S_("ld_cx"); ld_wo = S_("ld_wo"); t_op = S_("t_op"); v_oe = S_("v_oe")
    v_x1 = S_("v_x1"); st_x1 = S_("st_x1"); t_tr2 = S_("t_tr2"); v_te2 = S_("v_te2")
    v_st4 = S_("v_st4"); a_rs4 = S_("a_rs4")
    ld_gu = S_("ld_gu"); ld_wd = S_("ld_wd"); ld_x1 = S_("ld_x1")
    t_gu = S_("t_gu"); a_si = S_("a_si"); v_h = S_("v_h")
    t_dn = S_("t_dn"); v_de = S_("v_de"); t_tr5 = S_("t_tr5"); v_te5 = S_("v_te5")
    v_y = S_("v_y"); st_y = S_("st_y"); v_st5 = S_("v_st5"); a_rs5 = S_("a_rs5")

    def ln_stats(src_tile, d_free, v_done, a_done):
        p.op("vector", lambda e, s=src_tile: e.reduce_sum(stat[:, 0:1], s, axis=AXX), v_dr)
        p.wait("vector", v_dr, v_dr.n)
        p.op("vector", lambda e, d=d_free: e.tensor_scalar_mul(stat[:, 1:2], stat[:, 0:1], 1.0 / d), v_dr)
        p.op("scalar", lambda e, s=src_tile, d=d_free: e.activation(sqs[:, 0:d], s, AF.Square, accum_out=stat[:, 5:6]), a_sq)
        p.wait("vector", a_sq, a_sq.n)
        p.wait("vector", v_dr, v_dr.n)
        p.op("vector", lambda e: e.tensor_scalar(stat[:, 7:8], stat[:, 0:1], stat[:, 1:2], None, op0=ALU.mult), v_dr)
        p.wait("vector", v_dr, v_dr.n)
        p.op("vector", lambda e: e.tensor_sub(stat[:, 2:3], stat[:, 7:8], stat[:, 5:6]), v_done)
        p.wait("scalar", v_done, v_done.n)
        p.op("scalar", lambda e, d=d_free: e.activation(stat[:, 6:7], stat[:, 2:3], AF.Sqrt, bias=epsv[:, 0:1], scale=-1.0 / d), a_done)
        p.wait("vector", a_done, a_done.n)
        p.op("vector", lambda e: e.reciprocal(stat[:, 3:4], stat[:, 6:7]), v_dr)
        p.wait("vector", v_dr, v_dr.n)
        p.op("vector", lambda e: e.tensor_scalar(stat[:, 4:5], stat[:, 1:2], stat[:, 3:4], -1.0, op0=ALU.mult, op1=ALU.mult), v_done)
        p.wait("vector", v_done, v_done.n)

    p.dma("sync", ident_sb[:, :], ident_in, ld_c)
    p.op("vector", lambda e: e.memset(ones_rf[:, :], 1.0))
    p.op("vector", lambda e: e.memset(epsv[:, :], float(EPS)), v_ini)
    with (
        nc.sbuf_tensor("bo_f", [1, D], F32) as bo_f,
        nc.psum_tensor("pb1", [128, 512], F32) as pb1,
    ):
        p.dma("sync", bo_f[:, :], bo_in, ld_c)
        p.wait("tensor", ld_c, ld_c.n)
        p.wait("tensor", v_ini, 1)
        for j in range(4):
            p.op("tensor", lambda e, j=j: e.matmul(pb1[:, :], ones_rf[:, :], bo_f[:, 512 * j : 512 * j + 512], start=True, stop=True), t_b)
            p.wait("vector", t_b, 1 + j)
            p.op("vector", lambda e, j=j: e.tensor_copy(bot_r[:, 512 * j : 512 * j + 512], pb1[:, :]), v_b)
            if j < 3:
                p.wait("tensor", v_b, 1 + j)

    es_pc = ExitStack()
    pc_stg = es_pc.enter_context(nc.sbuf_tensor("pc_stg", [128, 16 * 128], F32))
    pc_bf = es_pc.enter_context(nc.sbuf_tensor("pc_bf", [128, 16 * 128], BF16))
    pcast_jobs = []

    def plan_precast(src, dst, rows, cols, col_blk):
        for c0 in range(0, cols, col_blk):
            w = min(col_blk, cols - c0)
            pcast_jobs.append((fm3(src[:, c0 : c0 + w]), fm3(dst.ap()[:, c0 : c0 + w]), (rows // 128) * w, rows // 128))

    plan_precast(wo_perm, wo_bfd, D, D, 128)
    plan_precast(w_gu, gu_bfd, D, 2 * INTER, 128)
    plan_precast(w_dT, wd_bfd, INTER, D, 32)
    NPC_WO = 16
    npc = len(pcast_jobs)
    p.wait("sync", t_b, 4)
    for j, (sview, dview, nel, nch) in enumerate(pcast_jobs):
        p.wait("sync", cs_pc, j)
        p.dma("sync", sb3(pc_stg[:, 0:nel], nch), sview, ld_pc)
        p.wait("gpsimd", ld_pc, 16 * (j + 1))
        p.wait("gpsimd", st_pc, 16 * j)
        p.op("gpsimd", lambda e, n=nel: e.tensor_copy(pc_bf[:, 0:n], pc_stg[:, 0:n]), cs_pc)
        p.wait("sync", cs_pc, j + 1)
        p.dma("sync", dview, sb3(pc_bf[:, 0:nel], nch), st_pc)

    es4 = ExitStack()
    xr = [es4.enter_context(nc.sbuf_tensor(f"xr4{i}", [128, D], F32)) for i in range(2)]
    ctxs = es4.enter_context(nc.sbuf_tensor("ctxs", [128, 16 * TOWN], BF16))
    wob = es4.enter_context(nc.sbuf_tensor("wob", [128, 16 * 512], BF16))
    attn = es4.enter_context(nc.sbuf_tensor("attn", [128, NTB * D], BF16))
    x1sb = [es4.enter_context(nc.sbuf_tensor(f"x1sb{i}", [128, D], BF16)) for i in range(2)]
    lntm2 = es4.enter_context(nc.sbuf_tensor("lntm2", [128, D], BF16))
    es4o = ExitStack()
    po_ = [es4o.enter_context(nc.psum_tensor(f"po{i}", [128, 512], F32)) for i in range(2)]

    for dc in range(16):
        p.dma("sync", ctxs[:, TOWN * dc : TOWN * dc + TOWN],
              ctx_own[128 * dc : 128 * dc + 128, :], ld_cx)

    p.wait("tensor", v_b, 4)
    for dspan in range(4):
        if dspan > 0:
            p.wait("sync", t_op, t_op.n)
            p.wait("tensor", v_oe, v_oe.n)
        p.wait("sync", st_pc, 16 * NPC_WO)
        p.dma("sync", sb3(wob[:, :], 16), fm3(wo_bfd.ap()[:, 512 * dspan : 512 * dspan + 512]), ld_wo)
        p.wait("tensor", ld_wo, 16 * (dspan + 1))
        if dspan == 0:
            p.wait("tensor", ld_cx, ld_cx.n)
        for tb in range(NTB):
            pb = tb % 2
            if v_oe.n >= 2:
                p.wait("tensor", v_oe, v_oe.n - 1)
            for dc in range(16):
                p.op("tensor", lambda e, dc=dc, tb=tb, pb=pb: e.matmul(
                    po_[pb][:, :], ctxs[:, TOWN * dc + 128 * tb : TOWN * dc + 128 * tb + 128],
                    wob[:, 512 * dc : 512 * dc + 512],
                    start=(dc == 0), stop=(dc == 15)), t_op if dc == 15 else None)
            p.wait("vector", t_op, t_op.n)
            p.op("vector", lambda e, tb=tb, dspan=dspan, pb=pb: e.tensor_add(
                attn[:, D * tb + 512 * dspan : D * tb + 512 * dspan + 512], po_[pb][:, :], bot_r[:, 512 * dspan : 512 * dspan + 512]), v_oe)

    es4o.close()
    es4p = ExitStack()
    ptr2 = [es4p.enter_context(nc.psum_tensor(f"ptr2{i}", [128, 128], BF16)) for i in range(2)]
    for tb in range(NTB):
        b = tb % 2
        if tb >= 2:
            p.wait("sync", v_x1, 2 * (tb - 1))
        p.dma("sync", xr[b][:, :], x_own[128 * tb : 128 * tb + 128, :], ld_x)
        at = attn[:, D * tb : D * tb + D]
        p.wait("vector", v_oe, v_oe.n)
        p.wait("scalar", v_oe, v_oe.n)
        ln_stats(at, D, v_st4, a_rs4)
        p.wait("vector", ld_x, 16 * (tb + 1))
        if tb >= 2:
            p.wait("vector", st_x1, 16 * (tb - 1))
        p.op("vector", lambda e, at=at: e.tensor_scalar(nrmt[:, :], at, stat[:, 3:4], stat[:, 4:5], op0=ALU.mult, op1=ALU.add), v_dr)
        p.wait("vector", v_dr, v_dr.n)
        p.op("vector", lambda e, b=b: e.tensor_add(x1sb[b][:, :], nrmt[:, :], xr[b][:, :]), v_x1)
        p.wait("scalar", v_x1, v_x1.n)
        p.dma("scalar", x1_dram.ap()[128 * tb : 128 * tb + 128, :], x1sb[b][:, :], st_x1)
        p.wait("scalar", v_x1, v_x1.n)
        ln_stats(x1sb[b][:, :], D, v_st4, a_rs4)
        p.op("vector", lambda e, b=b: e.tensor_scalar(lntm2[:, :], x1sb[b][:, :], stat[:, 3:4], stat[:, 4:5], op0=ALU.mult, op1=ALU.add), v_x1)
        p.wait("tensor", v_x1, v_x1.n)
        if tb == 0:
            p.wait("tensor", v_oe, v_oe.n)
        for dc in range(16):
            pb = dc % 2
            if t_tr2.n >= 2:
                p.wait("tensor", v_te2, v_te2.n - 1)
            p.op("tensor", lambda e, dc=dc, pb=pb: e.transpose(ptr2[pb][:, :], lntm2[:, 128 * dc : 128 * dc + 128], ident_sb[:, :]), t_tr2)
            p.wait("vector", t_tr2, t_tr2.n)
            p.op("vector", lambda e, dc=dc, tb=tb, pb=pb: e.tensor_copy(ln2t[:, dc * TOWN + 128 * tb : dc * TOWN + 128 * tb + 128], ptr2[pb][:, :]), v_te2)

    es4p.close()
    es4.close()
    es_pc.close()
    es5 = ExitStack()
    hsb = es5.enter_context(nc.sbuf_tensor("hsb", [128, 64 * 512], BF16))
    gur = [es5.enter_context(nc.sbuf_tensor(f"gur{i}", [128, 16 * 256], BF16)) for i in range(2)]
    wdr = [es5.enter_context(nc.sbuf_tensor(f"wdr{i}", [128, 32 * 128], BF16)) for i in range(2)]
    sil = es5.enter_context(nc.sbuf_tensor("sil", [128, 512], F32))
    ffm = [es5.enter_context(nc.sbuf_tensor(f"ffm{i}", [128, 512], BF16)) for i in range(2)]
    ftm = es5.enter_context(nc.sbuf_tensor("ftm", [128, 4 * D], BF16))
    x1r = [es5.enter_context(nc.sbuf_tensor(f"x1r{i}", [128, D], BF16)) for i in range(2)]
    yt = [es5.enter_context(nc.sbuf_tensor("yt0", [128, D], F32))] * 2
    pg = es5.enter_context(nc.psum_tensor("pg", [128, 512], F32))
    pu = es5.enter_context(nc.psum_tensor("pu", [128, 512], F32))
    pd = [es5.enter_context(nc.psum_tensor(f"pd{i}", [128, 512], F32)) for i in range(2)]
    ptr5 = [es5.enter_context(nc.psum_tensor(f"ptr5{i}", [128, 128], BF16)) for i in range(2)]

    p.wait("sync", v_te2, v_te2.n)
    p.wait("sync", st_pc, 16 * npc)
    p.wait("tensor", v_te2, v_te2.n)
    for span in range(2):
        for ic in range(64):
            b = ic % 2
            if span * 64 + ic >= 2:
                p.wait("sync", t_gu, t_gu.n)
            p.dma("sync", sb3(gur[b][:, :], 16), fm3(gu_bfd.ap()[:, 256 * ic : 256 * ic + 256]), ld_gu)
            p.wait("tensor", ld_gu, 16 * (span * 64 + ic + 1))
            if span > 0 and ic == 0:
                p.wait("vector", t_dn, t_dn.n)
            if span * 64 + ic >= 1:
                p.wait("tensor", v_h, v_h.n)
            for dc in range(16):
                p.op("tensor", lambda e, b=b, dc=dc, span=span: e.matmul(
                    pg[:, :], gur[b][:, 256 * dc : 256 * dc + 128],
                    ln2t[:, dc * TOWN + 512 * span : dc * TOWN + 512 * span + 512],
                    start=(dc == 0), stop=(dc == 15)))
            for dc in range(16):
                p.op("tensor", lambda e, b=b, dc=dc, span=span: e.matmul(
                    pu[:, :], gur[b][:, 256 * dc + 128 : 256 * dc + 256],
                    ln2t[:, dc * TOWN + 512 * span : dc * TOWN + 512 * span + 512],
                    start=(dc == 0), stop=(dc == 15)), t_gu if dc == 15 else None)
            p.wait("scalar", t_gu, t_gu.n)
            p.op("scalar", lambda e: e.activation(sil[:, :], pg[:, :], AF.Silu), a_si)
            p.wait("vector", a_si, a_si.n)
            p.op("vector", lambda e, ic=ic: e.tensor_tensor(hsb[:, 512 * ic : 512 * ic + 512], sil[:, :], pu[:, :], op=ALU.mult), v_h)
        if span > 0:
            p.wait("vector", a_rs5, a_rs5.n)
            p.wait("vector", v_y, v_y.n)
        for dc in range(16):
            db = dc % 2
            if span * 16 + dc >= 1:
                p.wait("tensor", v_de, v_de.n)
            for half in range(2):
                hidx = (span * 16 + dc) * 2 + half
                b = hidx % 2
                if hidx >= 2:
                    p.wait("sync", t_dn, t_dn.n)
                p.dma("sync", sb3(wdr[b][:, :], 32),
                      fm3(wd_bfd.ap()[4096 * half : 4096 * half + 4096, 128 * dc : 128 * dc + 128]), ld_wd)
                p.wait("tensor", ld_wd, 16 * (hidx + 1))
                for ic in range(32):
                    icg = 32 * half + ic
                    p.op("tensor", lambda e, b=b, ic=ic, icg=icg, db=db: e.matmul(
                        pd[db][:, :], wdr[b][:, 128 * ic : 128 * ic + 128],
                        hsb[:, 512 * icg : 512 * icg + 512],
                        start=(icg == 0), stop=(icg == 63)), t_dn if icg == 63 else None)
            p.wait("vector", t_dn, t_dn.n)
            if t_tr5.n >= 8:
                p.wait("vector", t_tr5, t_tr5.n - 4)
            p.op("vector", lambda e, b=(2 * (span * 16 + dc) + 1) % 2, db=dc % 2: e.tensor_copy(ffm[db][:, :], pd[db][:, :]), v_de)
            p.wait("tensor", v_de, v_de.n)
            for j in range(4):
                pb = j % 2
                if t_tr5.n >= 2:
                    p.wait("tensor", v_te5, v_te5.n - 1)
                p.op("tensor", lambda e, j=j, pb=pb, db=dc % 2: e.transpose(ptr5[pb][:, :], ffm[db][:, 128 * j : 128 * j + 128], ident_sb[:, :]), t_tr5)
                p.wait("vector", t_tr5, t_tr5.n)
                p.op("vector", lambda e, j=j, dc=dc, pb=pb: e.tensor_copy(ftm[:, D * j + 128 * dc : D * j + 128 * dc + 128], ptr5[pb][:, :]), v_te5)
        for t4 in range(4):
            tb = 4 * span + t4
            b = t4 % 2
            if tb >= 2:
                p.wait("sync", v_y, v_y.n - 1)
            p.wait("sync", st_x1, 16 * (tb + 1))
            p.dma("sync", x1r[b][:, :], x1_dram.ap()[128 * tb : 128 * tb + 128, :], ld_x1)
            ft = ftm[:, D * t4 : D * t4 + D]
            p.wait("vector", v_te5, v_te5.n)
            p.wait("scalar", v_te5, v_te5.n)
            ln_stats(ft, D, v_st5, a_rs5)
            p.wait("vector", ld_x1, 16 * (tb + 1))
            if tb >= 1:
                p.wait("vector", st_y, 16 * tb)
            p.op("vector", lambda e, ft=ft: e.tensor_scalar(nrmt[:, :], ft, stat[:, 3:4], stat[:, 4:5], op0=ALU.mult, op1=ALU.add), v_dr)
            p.wait("vector", v_dr, v_dr.n)
            p.op("vector", lambda e, b=b: e.tensor_add(yt[b][:, :], nrmt[:, :], x1r[b][:, :]), v_y)
            p.wait("scalar", v_y, v_y.n)
            p.dma("scalar", y[128 * tb : 128 * tb + 128, :], yt[b][:, :], st_y)

    p.wait("sync", st_y, st_y.n)
    return _finish(nc, p, es, es5)


def _finish(nc, p, es, *extra):
    with allow_ici_all_to_all(), nc.Block() as block:
        block.sync(lambda e: [fn(e) for fn in p.ops["sync"]])
        block.scalar(lambda e: [fn(e) for fn in p.ops["scalar"]])
        block.vector(lambda e: [fn(e) for fn in p.ops["vector"]])
        block.tensor(lambda e: [fn(e) for fn in p.ops["tensor"]])
        block.gpsimd(lambda e: [fn(e) for fn in p.ops["gpsimd"]])
    for x in extra:
        x.close()
    es.close()
    return nc


_slopes = (2.0 ** (-8.0 / H)) ** np.arange(1, H + 1, dtype=np.float64)


def _host_inputs(inputs):
    x = np.asarray(inputs["L0_attn_x"]).reshape(S, D)
    Wq = np.asarray(inputs["L0_attn_Wq"]); bq = np.asarray(inputs["L0_attn_bq"])
    Wk = np.asarray(inputs["L0_attn_Wk"]); bk = np.asarray(inputs["L0_attn_bk"])
    Wv = np.asarray(inputs["L0_attn_Wv"]); bv = np.asarray(inputs["L0_attn_bv"])
    Wo = np.asarray(inputs["L0_attn_Wo"]); bo = np.asarray(inputs["L0_attn_bo"])
    Wgu = np.asarray(inputs["L0_ffn_w_gate_up"])
    Wd = np.asarray(inputs["L0_ffn_w_down"])

    perm = []
    for i in range(NC):
        perm += list(range(HD * i, HD * i + HD)) + list(range(HD * (15 - i), HD * (15 - i) + HD))
    wo_perm = np.ascontiguousarray(Wo[perm, :], dtype=np.float32)
    wg = np.ascontiguousarray(Wgu[:INTER].T)
    wu = np.ascontiguousarray(Wgu[INTER:].T)
    w_gu = np.empty((D, 2 * INTER), dtype=np.float32)
    w_gu.reshape(D, 64, 2, 128)[:, :, 0, :] = wg.reshape(D, 64, 128)
    w_gu.reshape(D, 64, 2, 128)[:, :, 1, :] = wu.reshape(D, 64, 128)
    w_dT = np.ascontiguousarray(Wd.T, dtype=np.float32)

    ident = np.eye(128, dtype=ml_dtypes.bfloat16)
    masktri = np.where(np.arange(128)[:, None] < np.arange(128)[None, :],
                       np.float32(MASKVAL), np.float32(0.0)).astype(ml_dtypes.bfloat16)

    in_maps = []
    for r in range(NC):
        h0, h1 = r, 15 - r
        cols = list(range(HD * h0, HD * h0 + HD)) + list(range(HD * h1, HD * h1 + HD))
        al = np.empty((128, 2), dtype=np.float32)
        dl = np.zeros((1088, 1024), dtype=ml_dtypes.bfloat16)
        row = 0
        for hli, h in enumerate((h0, h1)):
            sl = float(_slopes[h])
            al[:, hli] = (sl * np.arange(128)).astype(np.float32)
            for c in range(16):
                for kt in range(4 * (c + 1)):
                    q = 512 * c + np.arange(512)
                    v64 = SQ * (sl * (128 * kt - q) - CSH)
                    hi = v64.astype(ml_dtypes.bfloat16)
                    lo = (v64 - hi.astype(np.float64)).astype(ml_dtypes.bfloat16)
                    dl[row, 0:512] = hi
                    dl[row, 512:1024] = lo
                    row += 1
        assert row == 1088
        in_maps.append({
            "x_own": np.ascontiguousarray(x[TOWN * r : TOWN * r + TOWN, :], dtype=np.float32),
            "wq_own": np.ascontiguousarray(Wq[:, cols], dtype=np.float32),
            "wk_own": np.ascontiguousarray(Wk[:, cols], dtype=np.float32),
            "wv_own": np.ascontiguousarray(Wv[:, cols], dtype=np.float32),
            "bqk": np.ascontiguousarray(np.stack(
                [bq[cols[:128]], bq[cols[128:]], bk[cols[:128]], bk[cols[128:]]], axis=1), dtype=np.float32),
            "bv_own": np.ascontiguousarray(bv[cols][None, :], dtype=np.float32),
            "bo_in": np.ascontiguousarray(bo[None, :], dtype=np.float32),
            "wo_perm": wo_perm,
            "w_gu": w_gu,
            "w_dT": w_dT,
            "alibi": al,
            "delta": dl,
            "masktri": masktri,
            "ident_in": ident,
        })
    return in_maps


_B_KEYS = ("x_own", "bo_in", "wo_perm", "w_gu", "w_dT", "ident_in")


def kernel(**inputs):
    in_maps = _host_inputs(inputs)
    nca = build_program(part="A")
    res_a = _bu.run_bass_kernel_spmd(nca, in_maps, core_ids=list(range(NC)))
    yctx = [np.asarray(res_a.results[i]["yctx"]) for i in range(NC)]
    in_maps_b = []
    for r in range(NC):
        m = {k: in_maps[r][k] for k in _B_KEYS}
        m["ctx_own"] = np.ascontiguousarray(
            np.concatenate([yctx[i][256 * r : 256 * r + 256, :] for i in range(NC)], axis=0))
        in_maps_b.append(m)
    ncb = build_b()
    res_b = _bu.run_bass_kernel_spmd(ncb, in_maps_b, core_ids=list(range(NC)))
    out = np.concatenate([res_b.results[r]["y"] for r in range(NC)], axis=0)
    return out.reshape(1, S, D).astype(np.float32)
